# version 15
# baseline (speedup 1.0000x reference)
import os
import contextlib
import numpy as np

import concourse.bass as bass
import concourse.mybir as mybir
from concourse.bass_utils import run_bass_kernel_spmd

F32 = mybir.dt.float32
BF16 = mybir.dt.bfloat16
I32 = mybir.dt.int32
AF = mybir.ActivationFunctionType
ALU = mybir.AluOpType

DBG = os.environ.get("MK_DEBUG", "") != ""
STAGE = int(os.environ.get("MK_STAGE", "9"))
NOAG = os.environ.get("MK_NOAG", "") != ""

B, L, DM = 2, 4096, 4096
NCORES = 8
TPC = 1024
CH = 128
NCH = TPC // CH
H, KVH, HD = 32, 8, 128
GQ = H // KVH
NI = DM // 128
INTER = 11008
NKI = INTER // 128
HG = 4
NHG = H // HG
RMS_EPS = 1e-5
LN_EPS = 1e-5
AGR = H * HD + 1
CVALS = (1e-5, 1.0, -1.0, 1.0 / 128.0)

ENGS = ("tensor", "vector", "scalar", "gpsimd", "sync")
DQS = ("gpsimd", "sync", "scalar")


class Sched:

    BATCH = {"CONST", "gt8", "ygd", "x2d", "hd", "ag_in", "outd", "ygres", "dbg", "vtd", "sgd", "dcyd"}

    def __init__(self, nc, stack):
        self.nc = nc
        self.esem = {e: stack.enter_context(nc.semaphore("Se" + e)) for e in ENGS}
        self.agsem = stack.enter_context(nc.semaphore("Sag"))
        self.dpool = [stack.enter_context(nc.semaphore(f"Sm{i}")) for i in range(70)]
        self.dstream = {}
        self.ecnt = {e: 0 for e in ENGS}
        self.agcnt = 0
        self.wr = {}
        self.rd = {}
        self.seen = {e: {} for e in ENGS}
        self.pend = {e: [] for e in ENGS}
        self.stop = False

    def _stream(self, s):
        if s not in self.dstream:
            assert len(self.dstream) < len(self.dpool), "DMA sem pool exhausted"
            self.dstream[s] = [self.dpool[len(self.dstream)], 0]
        return self.dstream[s]

    def _semof(self, tok):
        kind = tok[0]
        if kind == "e":
            return "e" + tok[1], self.esem[tok[1]], tok[2]
        if kind == "m":
            s = tok[1]
            ent = self.dstream[s]
            val = ent[1] if s.split("@")[0] in self.BATCH else tok[2]
            return "m" + s, ent[0], val
        return "ag", self.agsem, tok[1]

    def _waits(self, eng, reads, writes):
        need = {}
        if getattr(self, "_barrier_pending", None) and self._barrier_pending.get(eng):
            self._barrier_pending[eng] = False
            for tok in self._barrier:
                if tok[0] == "e" and tok[1] == eng:
                    continue
                key, sem, cnt = self._semof(tok)
                if self.seen[eng].get(key, 0) < cnt:
                    need[key] = (sem, cnt)
                    self.seen[eng][key] = cnt

        def add(tok):
            if tok is None:
                return
            if tok[0] == "e" and tok[1] == eng and eng == "tensor":
                return
            key, sem, cnt = self._semof(tok)
            if self.seen[eng].get(key, 0) >= cnt:
                return
            if key not in need or need[key][1] < cnt:
                need[key] = (sem, cnt)

        for k in reads:
            add(self.wr.get(k))
        for k in writes:
            add(self.wr.get(k))
            for t in self.rd.get(k, ()):
                add(t)
        for key, (_s, c) in need.items():
            self.seen[eng][key] = max(self.seen[eng].get(key, 0), c)
        return list(need.values())

    def _record(self, reads, writes, tok):
        for k in reads:
            self.rd.setdefault(k, []).append(tok)
        for k in writes:
            self.wr[k] = tok
            self.rd[k] = []

    def op(self, eng, fn, reads=(), writes=()):
        if self.stop:
            return None
        waits = self._waits(eng, reads, writes)
        self.ecnt[eng] += 1
        tok = ("e", eng, self.ecnt[eng])
        self.pend[eng].append((waits, [fn], self.esem[eng], 1))
        self._record(reads, writes, tok)
        return tok

    def mms(self, fns, reads=(), writes=()):
        if self.stop:
            return None
        waits = self._waits("tensor", reads, writes)
        self.ecnt["tensor"] += 1
        tok = ("e", "tensor", self.ecnt["tensor"])
        self.pend["tensor"].append((waits, list(fns), self.esem["tensor"], 1))
        self._record(reads, writes, tok)
        return tok

    def dma(self, q, fn, reads=(), writes=(), stream=None, nodep=()):
        if self.stop:
            return None
        s = (stream if stream is not None else (list(writes) + list(nodep))[0]) + "@" + q
        ent = self._stream(s)
        waits = self._waits(q, reads, writes)
        ent[1] += 16
        tok = ("m", s, ent[1])
        self.pend[q].append((waits, [fn], ent[0], 16))
        self._record(reads, list(writes) + list(nodep), tok)
        return tok

    def collective(self, fn, reads=(), writes=()):
        if self.stop:
            return None
        waits = self._waits("gpsimd", reads, writes)
        self.agcnt += 1
        tok = ("ag", self.agcnt)
        self.pend["gpsimd"].append((waits, [fn], self.agsem, None))
        self._record(reads, writes, tok)
        return tok

    def raw(self, eng, fn):
        if self.stop:
            return
        self.pend[eng].append(([], [fn], None, 0))

    def barrier(self):
        if self.stop:
            return
        toks = []
        for e in ENGS:
            if self.ecnt[e]:
                toks.append(("e", e, self.ecnt[e]))
        for s, ent in self.dstream.items():
            if ent[1]:
                toks.append(("m", s, ent[1]))
        self._barrier = toks
        self._barrier_pending = {e: True for e in ENGS}

    def flush(self, block):
        for eng in ENGS:
            ops = self.pend[eng]
            if not ops:
                continue
            self.pend[eng] = []

            def body(E, ops=ops):
                for waits, fns, sem, inc in ops:
                    for s, c in waits:
                        E.wait_ge(s, c)
                    inst = None
                    for f in fns:
                        inst = f(E)
                    if sem is not None:
                        if inc is None:
                            inst.then_inc(sem)
                        elif inc:
                            inst.then_inc(sem, inc)
            getattr(block, eng)(body)


def build_nc():
    nc = bass.Bass("TRN2", target_bir_lowering=False)

    def inp(name, shape, dtype=F32):
        return nc.dram_tensor(name, list(shape), dtype, kind="ExternalInput").ap()

    x = inp("x", (TPC, DM))
    WqT = inp("WqT", (DM, DM))
    WkT = inp("WkT", (KVH * HD, DM))
    WvT = inp("WvT", (2048, 2048))
    WgT = inp("WgT", (8192, 2048))
    WoT = inp("WoT", (8192, 2048))
    WinT = inp("WinT", (128, 32 * H))
    WgaT = inp("WgaT", (INTER, DM), I32)
    WupT = inp("WupT", (INTER, DM), I32)
    WdnT = inp("WdnT", (44032, 1024), I32)
    sgate2 = inp("sgate2", (128, NKI))
    sup2 = inp("sup2", (128, NKI))
    sdownb = inp("sdownb", (128, DM))
    gnormr = inp("gnormr", (128, 512))
    dtb = inp("dtb", (H, 1))
    Alb = inp("Alb", (H, 1))
    cosr = inp("cosr", (128, TPC))
    sinr = inp("sinr", (128, TPC))
    cmask = inp("cmask", (NCORES, H))
    identf_h = inp("identf_h", (128, 128))
    maskbig4_h = inp("maskbig4_h", (128, 512))
    ones32_h = inp("ones32_h", (32, CH))
    oh4_h = inp("oh4_h", (NHG * 32, 512))
    oh8_h = inp("oh8_h", (8, 8 * 128))
    trig8_h = inp("trig8_h", (8, 8))
    cpool_h = inp("cpool_h", (128, 4))

    out = nc.dram_tensor("out", [TPC, DM], F32, kind="ExternalOutput").ap()

    dkind = "ExternalOutput" if DBG else "Internal"
    x2_dram = nc.dram_tensor("x2_dram", [TPC, DM], F32, kind=dkind).ap()
    ygT_dram = nc.dram_tensor("ygT_dram", [DM, TPC], BF16, kind=dkind).ap()
    h_dram = nc.dram_tensor("h_dram", [INTER, TPC], BF16, kind="Internal").ap()
    vt_dram = nc.dram_tensor("vt_dram", [128, NCH * KVH * HD], BF16, kind="Internal").ap()
    sg_dram = nc.dram_tensor("sg_dram", [128, NHG * NCH * 512], BF16, kind="Internal").ap()
    dcy_dram = nc.dram_tensor("dcy_dram", [128, NHG * NCH * 512], BF16, kind="Internal").ap()
    ag_in_t = nc.dram_tensor("ag_in", [AGR, HD], F32, kind="Internal")
    ag_out_t = nc.dram_tensor("ag_out", [NCORES * AGR, HD], F32, kind="Internal")
    ag_in = ag_in_t.ap()
    ag_out = ag_out_t.ap()
    if DBG:
        dbg_xn = nc.dram_tensor("dbg_xn", [TPC, DM], F32, kind="ExternalOutput").ap()
        dbg_kfm = nc.dram_tensor("dbg_kfm", [128, KVH * TPC], F32, kind="ExternalOutput").ap()
        dbg_vtok = nc.dram_tensor("dbg_vtok", [128, NCH * KVH * HD], F32, kind="ExternalOutput").ap()
        dbg_GT = nc.dram_tensor("dbg_GT", [128, NCH * H], F32, kind="ExternalOutput").ap()
        dbg_dtT = nc.dram_tensor("dbg_dtT", [128, NCH * H], F32, kind="ExternalOutput").ap()
        dbg_q = nc.dram_tensor("dbg_q", [NHG * 128, HG * TPC], F32, kind="ExternalOutput").ap()
        dbg_sg = nc.dram_tensor("dbg_sg", [128, NHG * NCH * 512], F32, kind="ExternalOutput").ap()
        dbg_Sin = nc.dram_tensor("dbg_Sin", [NHG * 128, HG * HD], F32, kind="ExternalOutput").ap()

    with contextlib.ExitStack() as st:
        S = Sched(nc, st)

        def sb(name, shape, dtype):
            return st.enter_context(nc.sbuf_tensor(name, list(shape), dtype))

        identf = sb("identf", (128, 128), F32)
        identb = sb("identb", (128, 128), BF16)
        maskbig4 = sb("maskbig4", (128, 512), F32)
        ones32 = sb("ones32", (32, CH), F32)
        trig8 = sb("trig8", (8, 8), F32)
        m8 = sb("m8", (8, H), F32)
        dtbc = sb("dtbc", (H, 1), F32)
        Albc = sb("Albc", (H, 1), F32)
        RB = sb("RB", (128, NI * TPC), BF16)
        cpool = sb("cpool", (128, 4), F32)
        psA = [st.enter_context(nc.psum_tensor(f"psA{i}", [128, 512], F32)) for i in range(7)]
        psT = st.enter_context(nc.psum_tensor("psT", [128, 1024], BF16))
        PK = [f"psA{i}" for i in range(7)]

        def rep4(tile):
            return tile.rearrange("p (o e) -> p o e", o=1).broadcast_to([128, 4, 128])

        def bc4(tab, n, hg):
            c0 = n * 32 + 4 * hg
            return tab[:, c0:c0 + 4].rearrange("p (h o) -> p h o", o=1).broadcast_to([128, 4, 128])

        def v34(tile):
            return tile.rearrange("p (h e) -> p h e", h=4)

        block = st.enter_context(nc.Block())

        S.dma("sync", lambda E: E.dma_start(out=identf[:, :], in_=identf_h[:, :]), writes=["identf"], stream="CONST")
        S.dma("gpsimd", lambda E: E.dma_start(out=identb[:, :], in_=identf_h[:, :]), writes=["identb"], stream="CONST")
        S.dma("sync", lambda E: E.dma_start(out=maskbig4[:, :], in_=maskbig4_h[:, :]), writes=["maskbig"], stream="CONST")
        S.dma("sync", lambda E: E.dma_start(out=ones32[:, :], in_=ones32_h[:, :]), writes=["ones32"], stream="CONST")
        S.dma("sync", lambda E: E.dma_start(out=trig8[:, :], in_=trig8_h[:, :]), writes=["trig8"], stream="CONST")
        S.dma("sync", lambda E: E.dma_start(out=m8[:, :], in_=cmask[:, :]), writes=["m8"], stream="CONST")
        S.dma("sync", lambda E: E.dma_start(out=dtbc[:, :], in_=dtb[:, :]), writes=["dtbc"], stream="CONST")
        S.dma("sync", lambda E: E.dma_start(out=Albc[:, :], in_=Alb[:, :]), writes=["Albc"], stream="CONST")
        S.dma("sync", lambda E: E.dma_start(out=cpool[:, :], in_=cpool_h[:, :]), writes=["cpool"], stream="CONST")
        for j, v in enumerate(CVALS):
            nc.const_aps.register(F32, float(v), cpool[:, j:j + 1])

        def ln_transpose(src_dram, src_key, xt, xnt, cols, dst_key, kp):
            for tc in range(NCH):
                bi = tc % 2
                xk = f"{kp}xt{bi}"
                S.dma("sync", lambda E, tc=tc, bi=bi: E.dma_start(
                    out=xt[bi][:, :], in_=src_dram[tc * 128:(tc + 1) * 128, :]),
                    reads=[src_key] if src_key else [], writes=[xk])
                S.op("vector", lambda E, bi=bi: E.reduce_sum(cols[:, 0:1], xt[bi][:, :], axis=mybir.AxisListType.X),
                     reads=[xk], writes=[kp + "c0"])
                S.op("scalar", lambda E, bi=bi: E.activation(out=xnt[:, :], in_=xt[bi][:, :], func=AF.Square,
                                                             accum_out=cols[:, 1:2]),
                     reads=[xk], writes=["xnt" + kp, kp + "c1"])
                S.op("vector", lambda E: E.tensor_scalar_mul(cols[:, 2:3], cols[:, 0:1], 1.0 / DM),
                     reads=[kp + "c0"], writes=[kp + "mean"])
                S.op("scalar", lambda E: E.activation(out=cols[:, 3:4], in_=cols[:, 2:3], func=AF.Square),
                     reads=[kp + "mean"], writes=[kp + "m2"])
                S.op("vector", lambda E: E.tensor_scalar_mul(cols[:, 4:5], cols[:, 1:2], 1.0 / DM),
                     reads=[kp + "c1"], writes=[kp + "sq"])
                S.op("vector", lambda E: E.tensor_sub(cols[:, 5:6], cols[:, 4:5], cols[:, 3:4]),
                     reads=[kp + "sq", kp + "m2"], writes=[kp + "var"])
                S.op("scalar", lambda E: E.activation(out=cols[:, 6:7], in_=cols[:, 5:6], func=AF.Sqrt,
                                                      bias=float(LN_EPS)),
                     reads=[kp + "var", "cpool"], writes=[kp + "std"])
                S.op("vector", lambda E: E.reciprocal(cols[:, 7:8], cols[:, 6:7]),
                     reads=[kp + "std"], writes=[kp + "rstd"])
                S.op("vector", lambda E: E.tensor_scalar_mul(cols[:, 2:3], cols[:, 2:3], -1.0),
                     reads=[kp + "mean"], writes=[kp + "mean"])
                S.op("vector", lambda E, bi=bi: E.tensor_scalar(out=xnt[:, :], in0=xt[bi][:, :],
                                                                scalar1=cols[:, 2:3], scalar2=cols[:, 7:8],
                                                                op0=ALU.add, op1=ALU.mult),
                     reads=[xk, kp + "mean", kp + "rstd"], writes=["xnt" + kp])
                if DBG and kp == "p1":
                    S.dma("gpsimd", lambda E, tc=tc: E.dma_start(out=dbg_xn[tc * 128:(tc + 1) * 128, :], in_=xnt[:, :]),
                          reads=["xnt" + kp], stream="dbg", nodep=["dbg"])
                for i in range(NI):
                    pb = psT[:, 0:512] if i % 2 == 0 else psA[0][:, 0:256].bitcast(BF16)
                    pk2 = "psT" if i % 2 == 0 else "psA0"
                    S.op("tensor", lambda E, i=i, pb=pb: E.transpose(
                        pb[:, 0:128], xnt[:, i * 128:(i + 1) * 128], identb[:, :]),
                        reads=["xnt" + kp, "identb"], writes=[pk2])
                    S.op("scalar", lambda E, i=i, tc=tc, pb=pb: E.copy(
                        out=RB[:, i * TPC + tc * 128: i * TPC + (tc + 1) * 128],
                        in_=pb[:, 0:128]),
                        reads=[pk2], writes=[dst_key])

        with contextlib.ExitStack() as L1:
            def sb1(name, shape, dtype):
                return L1.enter_context(nc.sbuf_tensor(name, list(shape), dtype))

            cosb = sb1("cosb", (128, TPC), F32)
            sinb = sb1("sinb", (128, TPC), F32)
            kfm = sb1("kfm", (128, KVH * TPC), BF16)
            negGT = sb1("negGT", (128, NCH * H), F32)
            expGT = sb1("expGT", (128, NCH * H), F32)
            edecT = sb1("edecT", (128, NCH * H), F32)
            dcolsB = sb1("dcolsB", (128, NCH * H), F32)
            cumPB = sb1("cumPB", (128, NCH * H), F32)
            dtT = sb1("dtT", (128, NCH * H), F32)
            Gfm = sb1("Gfm", (32, TPC), F32)
            SA = sb1("SA", (128, HG * HD), F32)
            SB_ = sb1("SB_", (128, HG * HD), F32)
            Sin = sb1("Sin", (128, HG * HD), F32)
            Sin2 = sb1("Sin2", (128, HG * HD), F32)
            wblk = [sb1(f"wblk{i}", (128, 32 * 128), BF16) for i in range(2)]
            wtile = [sb1(f"wtile{i}", (128, 2048), BF16) for i in range(4)]
            oh8b = sb1("oh8b", (8, 8 * 128), F32)
            coefB = sb1("coefB", (128, NCORES * H), F32)
            gt8 = sb1("gt8", (8, H), F32)
            cf8 = sb1("cf8", (8, H), F32)

            S.dma("sync", lambda E: E.dma_start(out=cosb[:, :], in_=cosr[:, :]), writes=["cos"], stream="CONST")
            S.dma("sync", lambda E: E.dma_start(out=sinb[:, :], in_=sinr[:, :]), writes=["sin"], stream="CONST")
            S.dma("sync", lambda E: E.dma_start(out=oh8b[:, :], in_=oh8_h[:, :]), writes=["oh8b"], stream="CONST")

            with contextlib.ExitStack() as L2:
                xt = [L2.enter_context(nc.sbuf_tensor(f"xt{i}", [128, DM], F32)) for i in range(2)]
                xnt = L2.enter_context(nc.sbuf_tensor("xnt", [128, DM], BF16))
                cols = L2.enter_context(nc.sbuf_tensor("cols", [128, 8], F32))
                ln_transpose(x, None, xt, xnt, cols, "p1big", "p1")
                S.flush(block)
            if STAGE <= 1:
                S.stop = True

            S.barrier()
            with contextlib.ExitStack() as L2:
                zdt = L2.enter_context(nc.sbuf_tensor("zdt", [32, TPC], F32))
                gfm = L2.enter_context(nc.sbuf_tensor("gfm", [32, TPC], F32))
                dfm = L2.enter_context(nc.sbuf_tensor("dfm", [32, TPC], F32))
                rot = L2.enter_context(nc.sbuf_tensor("rot", [128, 512], F32))
                t1b = L2.enter_context(nc.sbuf_tensor("t1b", [128, 512], F32))
                t2b = L2.enter_context(nc.sbuf_tensor("t2b", [128, 512], F32))
                sm8 = L2.enter_context(nc.sbuf_tensor("sm8", [32, 64], F32))
                edsfT = L2.enter_context(nc.sbuf_tensor("edsfT", [128, NCH * H], F32))
                vst = [L2.enter_context(nc.sbuf_tensor(f"vst{i}", [128, 512], BF16)) for i in range(2)]
                vld = [L2.enter_context(nc.sbuf_tensor(f"vldA{i}", [128, 128], BF16)) for i in range(3)]
                ktr = [L2.enter_context(nc.sbuf_tensor(f"ktrA{i}", [128, 128], BF16)) for i in range(2)]
                sm8T = L2.enter_context(nc.sbuf_tensor("sm8T", [8, 128], F32))
                agst = L2.enter_context(nc.sbuf_tensor("agst", [128, 1024], F32))
                xc4 = [L2.enter_context(nc.sbuf_tensor(f"xc4{i}", [128, 512], BF16)) for i in range(2)]
                kd4 = [L2.enter_context(nc.sbuf_tensor(f"kd4{i}", [128, 512], BF16)) for i in range(2)]

                for kv in range(KVH):
                    S.dma("gpsimd", lambda E, kv=kv: E.dma_start(
                        out=wblk[kv % 2][:, :], in_=WkT[kv * 128:(kv + 1) * 128, :]),
                        writes=[f"wblk{kv % 2}"])
                    for th in range(2):
                        pi = (kv * 2 + th) % 4
                        ps = psA[pi]
                        S.mms([(lambda E, i=i, th=th, ps=ps, kv=kv: E.matmul(
                            ps[:, :], wblk[kv % 2][:, i * 128:(i + 1) * 128],
                            RB[:, i * TPC + th * 512: i * TPC + th * 512 + 512],
                            start=(i == 0), stop=(i == NI - 1))) for i in range(NI)],
                            reads=[f"wblk{kv % 2}", "p1big"], writes=[PK[pi]])
                        S.op("vector", lambda E, ps=ps: E.tensor_scalar_mul(rot[0:64, :], ps[64:128, :], -1.0),
                             reads=[PK[pi]], writes=["rot"])
                        S.op("vector", lambda E, ps=ps: E.tensor_copy(out=rot[64:128, :], in_=ps[0:64, :]),
                             reads=[PK[pi]], writes=["rot2"])
                        S.op("vector", lambda E, ps=ps, th=th: E.tensor_mul(
                            t1b[:, :], ps[:, :], cosb[:, th * 512:th * 512 + 512]),
                            reads=[PK[pi], "cos"], writes=["t1b"])
                        S.op("vector", lambda E, th=th: E.tensor_mul(
                            t2b[:, :], rot[:, :], sinb[:, th * 512:th * 512 + 512]),
                            reads=["rot", "rot2", "sin"], writes=["t2b"])
                        S.op("vector", lambda E, kv=kv, th=th: E.tensor_add(
                            kfm[:, kv * TPC + th * 512: kv * TPC + th * 512 + 512], t1b[:, :], t2b[:, :]),
                            reads=["t1b", "t2b"], writes=["kfm"])

                for og in range(2):
                    for pas in range(2):
                        for i4 in range(NI // 4):
                            bi = i4 % 4
                            S.dma("gpsimd", lambda E, i4=i4, og=og, bi=bi: E.dma_start(
                                out=wtile[bi][:, :],
                                in_=WvT[(og * 8 + i4) * 128:(og * 8 + i4 + 1) * 128, :]),
                                writes=[f"wtile{bi}"])
                            for im in range(4):
                                i = i4 * 4 + im
                                S.mms([(lambda E, i=i, im=im, t=t, bi=bi: E.matmul(
                                    psA[t % 4][:, :], RB[:, i * TPC + t * 128: i * TPC + (t + 1) * 128],
                                    wtile[bi][:, im * 512:(im + 1) * 512], start=(i == 0), stop=(i == NI - 1)))
                                    for t in range(pas * 4, pas * 4 + 4)],
                                    reads=[f"wtile{bi}", "p1big"],
                                    writes=[PK[t % 4] for t in range(pas * 4, pas * 4 + 4)])
                        for t in range(pas * 4, pas * 4 + 4):
                            vb = t % 2
                            S.op("scalar", lambda E, t=t, vb=vb: E.copy(
                                out=vst[vb][:, :], in_=psA[t % 4][:, :]),
                                reads=[PK[t % 4]], writes=[f"vst{vb}"])
                            S.dma("sync", lambda E, t=t, og=og, vb=vb: E.dma_start(
                                out=vt_dram[:, (t * KVH + og * 4) * 128:(t * KVH + og * 4 + 4) * 128],
                                in_=vst[vb][:, :]),
                                reads=[f"vst{vb}"], nodep=["vtd"], stream="vtd")

                S.dma("gpsimd", lambda E: E.dma_start(
                    out=wblk[0][:, 0:32 * 32], in_=WinT[:, :]),
                    writes=["wblk0"])
                for th in range(2):
                    ps = psA[4 + th]
                    S.mms([(lambda E, i=i, th=th, ps=ps: E.matmul(
                        ps[0:32, :], wblk[0][:, i * 32:(i + 1) * 32],
                        RB[:, i * TPC + th * 512: i * TPC + th * 512 + 512],
                        start=(i == 0), stop=(i == NI - 1))) for i in range(NI)],
                        reads=["wblk0", "p1big"], writes=[PK[4 + th]])
                    S.op("vector", lambda E, th=th, ps=ps: E.tensor_scalar_add(
                        zdt[:, th * 512:th * 512 + 512], ps[0:32, :], dtbc[:, 0:1]),
                        reads=[PK[4 + th], "dtbc"], writes=["zdt"])
                S.op("scalar", lambda E: E.activation(out=gfm[:, :], in_=zdt[:, :], func=AF.Abs),
                     reads=["zdt"], writes=["gfm"])
                S.op("scalar", lambda E: E.activation(out=gfm[:, :], in_=gfm[:, :], func=AF.Exp, scale=-1.0),
                     reads=["gfm", "cpool"], writes=["gfm"])
                S.op("scalar", lambda E: E.activation(out=gfm[:, :], in_=gfm[:, :], func=AF.Ln, bias=1.0),
                     reads=["gfm", "cpool"], writes=["gfm"])
                S.op("vector", lambda E: E.tensor_scalar_max(dfm[:, :], zdt[:, :], 0.0),
                     reads=["zdt"], writes=["dfm"])
                S.op("vector", lambda E: E.tensor_add(dfm[:, :], dfm[:, :], gfm[:, :]),
                     reads=["dfm", "gfm"], writes=["dfm"])
                S.op("scalar", lambda E: E.activation(out=sm8[:, 40:41], in_=Albc[:, 0:1], func=AF.Exp),
                     reads=["Albc"], writes=["negA"])
                S.op("vector", lambda E: E.tensor_scalar_mul(sm8[:, 40:41], sm8[:, 40:41], -1.0),
                     reads=["negA"], writes=["negA"])
                S.op("vector", lambda E: E.tensor_scalar_mul(gfm[:, :], dfm[:, :], sm8[:, 40:41]),
                     reads=["dfm", "negA"], writes=["gfm"])
                for n in range(NCH):
                    S.op("vector", lambda E, n=n: E.tensor_tensor_scan(
                        out=Gfm[:, n * CH:(n + 1) * CH], data0=ones32[:, :], data1=gfm[:, n * CH:(n + 1) * CH],
                        initial=0.0, op0=ALU.mult, op1=ALU.add),
                        reads=["gfm", "ones32"], writes=["Gfm"])
                for n in range(NCH):
                    sl = n % 2
                    S.op("tensor", lambda E, n=n, sl=sl: E.transpose(
                        psA[6][:, sl * 128: sl * 128 + 32], Gfm[:, n * CH:(n + 1) * CH], identf[0:32, 0:32]),
                        reads=["Gfm", "identf"], writes=[PK[6]])
                    S.op("scalar", lambda E, n=n, sl=sl: E.copy(out=negGT[:, n * 32:(n + 1) * 32],
                                                                in_=psA[6][:, sl * 128: sl * 128 + 32]),
                        reads=[PK[6]], writes=["GT"])
                    S.op("tensor", lambda E, n=n, sl=sl: E.transpose(
                        psA[5][:, sl * 128: sl * 128 + 32], dfm[:, n * CH:(n + 1) * CH], identf[0:32, 0:32]),
                        reads=["dfm", "identf"], writes=[PK[5]])
                    S.op("scalar", lambda E, n=n, sl=sl: E.copy(out=dtT[:, n * 32:(n + 1) * 32],
                                                                in_=psA[5][:, sl * 128: sl * 128 + 32]),
                        reads=[PK[5]], writes=["dtT"])
                S.op("vector", lambda E: E.tensor_scalar_mul(negGT[:, :], negGT[:, :], -1.0),
                     reads=["GT"], writes=["negGT"])
                S.op("scalar", lambda E: E.activation(out=expGT[:, :], in_=negGT[:, :], func=AF.Exp, scale=-1.0),
                     reads=["negGT", "cpool"], writes=["expGT"])
                for n in range(NCH):
                    S.op("vector", lambda E, n=n: E.tensor_copy(out=sm8[:, n:n + 1],
                                                                in_=Gfm[:, n * CH + 127:n * CH + 128]),
                         reads=["Gfm"], writes=["Glast"])
                S.op("vector", lambda E: E.reduce_sum(sm8[:, 8:9], sm8[:, 0:8], axis=mybir.AxisListType.X),
                     reads=["Glast"], writes=["Gtot"])
                S.op("vector", lambda E: E.memset(sm8[:, 16:17], 0.0), writes=["Gsh"])
                S.op("vector", lambda E: E.tensor_copy(out=sm8[:, 17:24], in_=sm8[:, 0:7]),
                     reads=["Glast"], writes=["Gsh"])
                S.op("vector", lambda E: E.tensor_tensor_scan(out=sm8[:, 48:56], data0=ones32[:, 0:8],
                                                              data1=sm8[:, 16:24], initial=0.0,
                                                              op0=ALU.mult, op1=ALU.add),
                     reads=["Gsh", "ones32"], writes=["cumGx"])
                S.op("vector", lambda E: E.tensor_scalar(out=sm8[:, 24:32], in0=sm8[:, 48:56], scalar1=-1.0,
                                                         scalar2=sm8[:, 8:9], op0=ALU.mult, op1=ALU.add),
                     reads=["cumGx", "Gtot"], writes=["Sfx"])
                for (src0, src1, dst0, wkey, rkey) in (
                        (0, 8, 0, "sm8T", "Glast"), (48, 56, 32, "sm8T2", "cumGx"), (24, 32, 64, "sm8T3", "Sfx")):
                    S.op("tensor", lambda E, src0=src0, src1=src1: E.transpose(
                        psA[6][0:8, 0:32], sm8[:, src0:src1], identf[0:32, 0:32]),
                        reads=[rkey, "identf"], writes=[PK[6]])
                    S.op("scalar", lambda E, dst0=dst0: E.copy(out=sm8T[:, dst0:dst0 + 32], in_=psA[6][0:8, 0:32]),
                         reads=[PK[6]], writes=[wkey])
                S.op("tensor", lambda E: E.transpose(psA[6][0:1, 0:32], sm8[:, 8:9], identf[0:32, 0:32]),
                     reads=["Gtot", "identf"], writes=[PK[6]])
                S.op("scalar", lambda E: E.copy(out=sm8T[0:1, 96:128], in_=psA[6][0:1, 0:32]),
                     reads=[PK[6]], writes=["GtotT"])

                def bcast8(src, dst, dstkey, sub_negGT, srckeys):
                    fns = []
                    for n in range(NCH):
                        fns.append(lambda E, n=n: E.matmul(psA[6][:, n * 32:(n + 1) * 32],
                                                           oh8b[:, n * 128:(n + 1) * 128], src,
                                                           start=True, stop=True))
                    S.mms(fns, reads=["oh8b"] + srckeys, writes=[PK[6]])
                    if sub_negGT:
                        S.op("vector", lambda E: E.tensor_add(dst[:, :], psA[6][:, 0:256], negGT[:, :]),
                             reads=[PK[6], "negGT"], writes=[dstkey])
                        S.op("scalar", lambda E: E.activation(out=dst[:, :], in_=dst[:, :], func=AF.Exp),
                             reads=[dstkey], writes=[dstkey])
                    else:
                        S.op("scalar", lambda E: E.activation(out=dst[:, :], in_=psA[6][:, 0:256], func=AF.Exp),
                             reads=[PK[6]], writes=[dstkey])

                bcast8(sm8T[:, 0:32], dcolsB, "dcolsB", False, ["sm8T"])
                bcast8(sm8T[:, 0:32], edecT, "edecT", True, ["sm8T"])
                bcast8(sm8T[:, 64:96], edsfT, "edsfT", True, ["sm8T3"])
                bcast8(sm8T[:, 32:64], cumPB, "cumPB", False, ["sm8T2"])

                if DBG:
                    S.dma("gpsimd", lambda E: E.dma_start(out=dbg_kfm[:, :], in_=kfm[:, :]), reads=["kfm"], stream="dbg", nodep=["dbg"])
                    S.dma("gpsimd", lambda E: E.dma_start(out=dbg_vtok[:, :], in_=vt_dram[:, :]), reads=["vtd"], stream="dbg", nodep=["dbg"])
                    S.dma("sync", lambda E: E.dma_start(out=dbg_GT[:, :], in_=negGT[:, :]), reads=["negGT"], stream="dbg", nodep=["dbg"])
                    S.dma("sync", lambda E: E.dma_start(out=dbg_dtT[:, :], in_=dtT[:, :]), reads=["dtT"], stream="dbg", nodep=["dbg"])

                oh4p = L2.enter_context(nc.sbuf_tensor("oh4p", [32, 512], F32))
                Gm4p = [L2.enter_context(nc.sbuf_tensor(f"Gm4p{i}", [32, 512], F32)) for i in range(2)]
                dcyp = [L2.enter_context(nc.sbuf_tensor(f"dcyp{i}", [128, 512], F32)) for i in range(2)]
                dcyb = [L2.enter_context(nc.sbuf_tensor(f"dcyb{i}", [128, 512], BF16)) for i in range(2)]
                for hg in range(NHG):
                    S.dma("sync", lambda E, hg=hg: E.dma_start(
                        out=oh4p[:, :], in_=oh4_h[hg * 32:(hg + 1) * 32, :]), writes=["oh4p"])
                    for n in range(NCH):
                        di = n % 2
                        S.op("gpsimd", lambda E, n=n, di=di: E.tensor_tensor(
                            out=v34(Gm4p[di][:, :])[0:32], in0=Gfm[:, n * CH:(n + 1) * CH]
                            .rearrange("p (o e) -> p o e", o=1).broadcast_to([32, 4, 128]),
                            in1=v34(oh4p[:, :])[0:32], op=ALU.mult),
                            reads=["Gfm", "oh4p"], writes=[f"Gm4p{di}"])
                        S.mms([(lambda E, di=di: E.matmul(psA[4 + di][:, :], identf[:, :], maskbig4[:, :],
                                                          start=True, stop=False)),
                               (lambda E, di=di: E.matmul(psA[4 + di][:, :], ones32[:, :128], Gm4p[di][:, :],
                                                          start=False, stop=True))],
                              reads=["identf", "maskbig", f"Gm4p{di}", "ones32"], writes=[PK[4 + di]])
                        S.op("vector", lambda E, n=n, hg=hg, di=di: E.tensor_tensor(
                            out=v34(dcyp[di][:, :]), in0=v34(psA[4 + di][:, :]), in1=bc4(negGT, n, hg), op=ALU.add),
                            reads=[PK[4 + di], "negGT"], writes=[f"dcyp{di}"])
                        S.op("scalar", lambda E, di=di: E.activation(out=dcyb[di][:, :], in_=dcyp[di][:, :], func=AF.Exp),
                             reads=[f"dcyp{di}"], writes=[f"dcyb{di}"])
                        S.dma("sync", lambda E, hg=hg, n=n, di=di: E.dma_start(
                            out=dcy_dram[:, hg * 4096 + n * 512: hg * 4096 + (n + 1) * 512],
                            in_=dcyb[di][:, :]),
                            reads=[f"dcyb{di}"], nodep=["dcyd"], stream="dcyd")

                for hg in range(NHG):
                    kv = hg
                    for n in range(NCH):
                        ci = n % 2
                        vi = n % 3
                        S.dma("sync", lambda E, n=n, kv=kv, vi=vi: E.dma_start(
                            out=vld[vi][:, :], in_=vt_dram[:, (n * KVH + kv) * 128:(n * KVH + kv + 1) * 128]),
                            reads=["vtd"], writes=[f"vldA{vi}"])
                        S.op("tensor", lambda E, n=n, kv=kv: E.transpose(
                            psT[:, 0:128], kfm[:, kv * TPC + n * 128: kv * TPC + (n + 1) * 128], identb[:, :]),
                            reads=["kfm", "identb"], writes=["psT"])
                        S.op("scalar", lambda E, ci=ci: E.copy(out=ktr[ci][:, :], in_=psT[:, 0:128]),
                             reads=["psT"], writes=[f"ktrA{ci}"])
                        S.op("gpsimd", lambda E, n=n, hg=hg, ci=ci, vi=vi: E.tensor_tensor(
                            out=v34(xc4[ci][:, :]), in0=rep4(vld[vi][:, :]),
                            in1=bc4(dtT, n, hg), op=ALU.mult),
                            reads=[f"vldA{vi}", "dtT"], writes=[f"xc4{ci}"])
                        S.op("gpsimd", lambda E, n=n, hg=hg, ci=ci: E.tensor_tensor(
                            out=v34(kd4[ci][:, :]), in0=rep4(ktr[ci][:, :]),
                            in1=bc4(edsfT, n, hg), op=ALU.mult),
                            reads=[f"ktrA{ci}", "edsfT"], writes=[f"kd4{ci}"])
                        S.mms([(lambda E, n=n, ci=ci, hl=hl: E.matmul(
                            psA[hl][:, 0:128],
                            kd4[ci][:, hl * 128:(hl + 1) * 128], xc4[ci][:, hl * 128:(hl + 1) * 128],
                            start=(n == 0), stop=(n == NCH - 1))) for hl in range(HG)],
                            reads=[f"xc4{ci}", f"kd4{ci}"],
                            writes=[PK[hl] for hl in range(HG)])
                    for hl in range(HG):
                        S.op("vector", lambda E, hg=hg, hl=hl: E.tensor_copy(
                            out=agst[:, (hg % 2) * 512 + hl * 128:(hg % 2) * 512 + (hl + 1) * 128],
                            in_=psA[hl][:, 0:128]),
                            reads=[PK[hl]], writes=[f"agst{hg % 2}"])
                    if hg % 2 == 1:
                        h0 = (hg - 1) * HG
                        S.dma("sync", lambda E, h0=h0: E.dma_start(
                            out=ag_in[h0 * 128:(h0 + 8) * 128, :].rearrange("(a p) e -> p a e", p=128),
                            in_=agst[:, :].rearrange("p (a e) -> p a e", a=8)),
                            reads=["agst0", "agst1"], nodep=["ag_in"])
                S.dma("sync", lambda E: E.dma_start(out=ag_in[H * HD:H * HD + 1, 0:32], in_=sm8T[0:1, 96:128]),
                      reads=["GtotT"], nodep=["ag_in"])
                if not NOAG:
                    S.collective(lambda E: E.collective_compute(
                        "AllGather", ALU.bypass, replica_groups=[list(range(NCORES))],
                        ins=[ag_in_t.ap().opt()], outs=[ag_out_t.ap().opt()]),
                        reads=["ag_in"], writes=["ag_out"])
                S.flush(block)
            if STAGE <= 2:
                S.stop = True

            S.barrier()
            with contextlib.ExitStack() as L2:
                def sb2(name, shape, dtype):
                    return L2.enter_context(nc.sbuf_tensor(name, list(shape), dtype))
                qg2 = [sb2(f"qg{i}", (128, HG * TPC), BF16) for i in range(2)]
                sgl = [sb2(f"sgl{i}", (128, 512), BF16) for i in range(3)]
                sgst = [sb2("sgst0", (128, 512), BF16)]
                gnb = sb2("gnb", (128, 512), BF16)
                S.dma("gpsimd", lambda E: E.dma_start(out=gnb[:, :], in_=gnormr[:, :]), writes=["gnb"], stream="CONST")
                rot = sb2("rotm", (128, 512), F32)
                t1b = sb2("t1m", (128, 512), F32)
                t2b = sb2("t2m", (128, 512), F32)
                dcyl = [sb2(f"dcyl{i}", (128, 512), BF16) for i in range(3)]
                apt4 = [sb2(f"apt4{i}", (128, 512), BF16) for i in range(2)]
                xcm4 = [sb2(f"xcm4{i}", (128, 512), BF16) for i in range(2)]
                kdm4 = [sb2(f"kdm4{i}", (128, 512), BF16) for i in range(2)]
                ytk4 = sb2("ytk40", (128, 512), F32)
                s1b4 = sb2("s1b4", (128, 512), F32)
                ygb4x = [sb2(f"ygb4{i}", (128, 512), BF16) for i in range(2)]
                rcol4 = sb2("rcol4", (128, 12), F32)
                sdump = sb2("sdump", (128, 128), BF16)
                Sbf4 = sb2("Sbf4b", (128, 512), BF16)
                Sinsc4 = sb2("Sinsc4", (128, 512), BF16)
                Stmp = sb2("Stmp", (128, 512), F32)
                cg4 = [sb2(f"cg4{i}", (128, 512), F32) for i in range(2)]
                Cch = [sb2(f"Cch{i}", (128, 512), F32) for i in range(2)]
                ygs4 = [sb2("ygs40", (128, 512), BF16)]
                vldm = [sb2(f"vldm{i}", (128, 128), BF16) for i in range(3)]
                ktrm = [sb2(f"ktrm{i}", (128, 128), BF16) for i in range(2)]
                SinF = [Sin, Sin2]

                def proj_units(hg):
                    p = hg % 2
                    qg = qg2[p]
                    qk = f"qg{p}"
                    units = []
                    for hl in range(HG):
                        hh = hg * HG + hl
                        units.append(lambda hh=hh, hl=hl: S.dma("gpsimd", lambda E: E.dma_start(
                            out=wblk[hl % 2][:, :], in_=WqT[hh * 128:(hh + 1) * 128, :]),
                            writes=[f"wblk{hl % 2}"]))
                        p0, p1_ = (hl % 2) * 2, (hl % 2) * 2 + 1

                        def u_q(hl=hl, p0=p0, p1_=p1_, qg=qg, qk=qk):
                            S.mms([(lambda E, i=i, th=th, hl=hl: E.matmul(
                                psA[(p0, p1_)[th]][:, :], wblk[hl % 2][:, i * 128:(i + 1) * 128],
                                RB[:, i * TPC + th * 512: i * TPC + th * 512 + 512],
                                start=(i == 0), stop=(i == NI - 1))) for i in range(NI) for th in range(2)],
                                reads=[f"wblk{hl % 2}", "p1big"], writes=[PK[p0], PK[p1_]])

                        def u_rope(hl=hl, th=None, p0=p0, p1_=p1_, qg=qg, qk=qk):
                            for th in range(2):
                                pi = (p0, p1_)[th]
                                ps = psA[pi]
                                S.op("vector", lambda E, ps=ps: E.tensor_scalar_mul(rot[0:64, :], ps[64:128, :], -1.0),
                                     reads=[PK[pi]], writes=["rotm"])
                                S.op("vector", lambda E, ps=ps: E.tensor_copy(out=rot[64:128, :], in_=ps[0:64, :]),
                                     reads=[PK[pi]], writes=["rotm2"])
                                S.op("vector", lambda E, ps=ps, th=th: E.tensor_mul(
                                    t1b[:, :], ps[:, :], cosb[:, th * 512:th * 512 + 512]),
                                    reads=[PK[pi], "cos"], writes=["t1m"])
                                S.op("vector", lambda E, th=th: E.tensor_mul(
                                    t2b[:, :], rot[:, :], sinb[:, th * 512:th * 512 + 512]),
                                    reads=["rotm", "rotm2", "sin"], writes=["t2m"])
                                S.op("vector", lambda E, hl=hl, th=th, qg=qg: E.tensor_add(
                                    qg[:, hl * TPC + th * 512: hl * TPC + th * 512 + 512], t1b[:, :], t2b[:, :]),
                                    reads=["t1m", "t2m"], writes=[qk])
                        units.append(u_q)
                        units.append(u_rope)
                    for pas in range(2):
                        for i4 in range(NI // 4):
                            bi = (pas * 8 + i4) % 4

                            def u_g(i4=i4, hg=hg, bi=bi, pas=pas):
                                S.dma("gpsimd", lambda E: E.dma_start(
                                    out=wtile[bi][:, :],
                                    in_=WgT[(hg * 8 + i4) * 128:(hg * 8 + i4 + 1) * 128, :]),
                                    writes=[f"wtile{bi}"])
                                for im in range(4):
                                    i = i4 * 4 + im
                                    S.mms([(lambda E, i=i, im=im, t=t, bi=bi: E.matmul(
                                        psA[t % 4][:, :], RB[:, i * TPC + t * 128: i * TPC + (t + 1) * 128],
                                        wtile[bi][:, im * 512:(im + 1) * 512], start=(i == 0), stop=(i == NI - 1)))
                                        for t in range(pas * 4, pas * 4 + 4)],
                                        reads=[f"wtile{bi}", "p1big"],
                                        writes=[PK[t % 4] for t in range(pas * 4, pas * 4 + 4)])
                                if i4 == NI // 4 - 1:
                                    for t in range(pas * 4, pas * 4 + 4):
                                        S.op("scalar", lambda E, t=t: E.activation(
                                            out=t1b[:, :], in_=psA[t % 4][:, :], func=AF.Silu),
                                            reads=[PK[t % 4]], writes=["t1m"])
                                        S.op("vector", lambda E: E.tensor_mul(
                                            sgst[0][:, :], t1b[:, 0:512], gnb[:, :]),
                                            reads=["t1m", "gnb"], writes=["sgst0"])
                                        S.dma("sync", lambda E, hg=hg, t=t: E.dma_start(
                                            out=sg_dram[:, hg * 4096 + t * 512: hg * 4096 + (t + 1) * 512],
                                            in_=sgst[0][:, :]),
                                            reads=["sgst0"], nodep=["sgd"], stream="sgd")
                            units.append(u_g)
                    if hg == 0:
                        def u_coef():
                            for r in range(NCORES):
                                S.dma("sync", lambda E, r=r: E.dma_start(
                                    out=gt8[r:r + 1, :], in_=ag_out[r * AGR + H * HD: r * AGR + H * HD + 1, 0:32]),
                                    reads=["ag_out"], writes=["gt8"], stream="gt8")
                            S.op("vector", lambda E: E.tensor_mul(gt8[:, :], gt8[:, :], m8[:, :]),
                                 reads=["gt8", "m8"], writes=["gt8"])
                            S.mms([lambda E: E.matmul(psA[6][0:8, 0:32], trig8[:, :], gt8[:, :], start=True, stop=True)],
                                  reads=["trig8", "gt8"], writes=[PK[6]])
                            S.op("scalar", lambda E: E.activation(out=cf8[:, :], in_=psA[6][0:8, 0:32], func=AF.Exp),
                                 reads=[PK[6]], writes=["cf8"])
                            S.op("vector", lambda E: E.tensor_mul(cf8[:, :], cf8[:, :], m8[:, :]),
                                 reads=["cf8", "m8"], writes=["cf8"])
                            fns = []
                            for r in range(NCORES):
                                fns.append(lambda E, r=r: E.matmul(psA[6][:, r * 32:(r + 1) * 32],
                                                                   oh8b[:, r * 128:(r + 1) * 128], cf8[:, :],
                                                                   start=True, stop=True))
                            S.mms(fns, reads=["oh8b", "cf8"], writes=[PK[6]])
                            S.op("vector", lambda E: E.tensor_copy(out=coefB[:, :], in_=psA[6][:, 0:256]),
                                 reads=[PK[6]], writes=["coefB"])
                        units.append(u_coef)
                    for r in range(NCORES):
                        def u_c(r=r, hg=hg, p=p):
                            ci = r % 2
                            S.dma("sync", lambda E: E.dma_start(
                                out=cg4[ci][:, :].rearrange("p (hl e) -> p hl e", hl=4),
                                in_=ag_out[r * AGR + hg * 512: r * AGR + (hg + 1) * 512, :]
                                .rearrange("(hl p) e -> p hl e", p=128)),
                                reads=["ag_out"], writes=[f"cg4{ci}"])
                            c0 = r * 32 + 4 * hg
                            cb = coefB[:, c0:c0 + 4].rearrange("p (h o) -> p h o", o=1).broadcast_to([128, 4, 128])
                            if r == 0:
                                S.op("gpsimd", lambda E: E.tensor_tensor(
                                    out=v34(Cch[0][:, :]), in0=v34(cg4[ci][:, :]), in1=cb, op=ALU.mult),
                                    reads=[f"cg4{ci}", "coefB"], writes=["Cch0"])
                            else:
                                S.op("gpsimd", lambda E: E.tensor_tensor(
                                    out=v34(Cch[1][:, :]), in0=v34(cg4[ci][:, :]), in1=cb, op=ALU.mult),
                                    reads=[f"cg4{ci}", "coefB"], writes=["Cch1"])
                                accb, acck = (Cch[0], "Cch0") if r % 2 == 1 else (SinF[p], f"SinF{p}")
                                dstb, dk = (SinF[p], f"SinF{p}") if r % 2 == 1 else (Cch[0], "Cch0")
                                S.op("gpsimd", lambda E, dstb=dstb, accb=accb: E.tensor_add(
                                    dstb[:, :], Cch[1][:, :], accb[:, :]),
                                    reads=["Cch1", acck], writes=[dk])
                        units.append(u_c)
                    if DBG:
                        units.append(lambda hg=hg, p=p: (
                            S.dma("gpsimd", lambda E: E.dma_start(out=dbg_q[hg * 128:(hg + 1) * 128, :],
                                                                  in_=qg2[p][:, :]),
                                  reads=[f"qg{p}"], stream="dbg", nodep=["dbg"]),
                            S.dma("gpsimd", lambda E: E.dma_start(out=dbg_sg[:, hg * 4096:(hg + 1) * 4096],
                                                                  in_=sg_dram[:, hg * 4096:(hg + 1) * 4096]),
                                  reads=["sgd"], stream="dbg", nodep=["dbg"]),
                            S.dma("sync", lambda E: E.dma_start(out=dbg_Sin[hg * 128:(hg + 1) * 128, :],
                                                                in_=SinF[p][:, :]),
                                  reads=[f"SinF{p}"], stream="dbg", nodep=["dbg"])))
                    return units

                def mixer_iter_steps(hg, n):
                  def _stepA(hg=hg, n=n):
                    p = hg % 2
                    qg = qg2[p]
                    qk = f"qg{p}"
                    Sprev, Snext = (SA, SB_) if n % 2 == 0 else (SB_, SA)
                    spk, snk = ("SA", "SB_") if n % 2 == 0 else ("SB_", "SA")
                    kv = hg
                    it = n % 2
                    vi = n % 3
                    if n == 0:
                        S.op("vector", lambda E: E.memset(SA[:, :], 0.0), writes=["SA"])
                    ktile = kfm[:, kv * TPC + n * 128: kv * TPC + (n + 1) * 128]
                    S.dma("sync", lambda E, n=n, kv=kv, vi=vi: E.dma_start(
                        out=vldm[vi][:, :], in_=vt_dram[:, (n * KVH + kv) * 128:(n * KVH + kv + 1) * 128]),
                        reads=["vtd"], writes=[f"vldm{vi}"])
                    S.dma("sync", lambda E, n=n, hg=hg, vi=vi: E.dma_start(
                        out=sgl[vi][:, :], in_=sg_dram[:, hg * 4096 + n * 512: hg * 4096 + (n + 1) * 512]),
                        reads=["sgd"], writes=[f"sgl{vi}"])
                    S.mms([lambda E, ktile=ktile: E.transpose(
                        psA[5][:, 0:64].bitcast(BF16), ktile, identb[:, :])],
                        reads=["kfm", "identb"], writes=[PK[5]])
                    S.op("scalar", lambda E, it=it: E.copy(out=ktrm[it][:, :], in_=psA[5][:, 0:64].bitcast(BF16)),
                         reads=[PK[5]], writes=[f"ktrm{it}"])
                    S.dma("sync", lambda E, n=n, hg=hg, vi=vi: E.dma_start(
                        out=dcyl[vi][:, :], in_=dcy_dram[:, hg * 4096 + n * 512: hg * 4096 + (n + 1) * 512]),
                        reads=["dcyd"], writes=[f"dcyl{vi}"])
                    S.mms([(lambda E, hl=hl, n=n, ktile=ktile, qg=qg: E.matmul(
                        psA[5][:, hl * 128:(hl + 1) * 128], ktile,
                        qg[:, hl * TPC + n * 128: hl * TPC + (n + 1) * 128],
                        start=True, stop=True)) for hl in range(HG)],
                        reads=["kfm", qk], writes=[PK[5]])
                    S.op("vector", lambda E, it=it, vi=vi: E.tensor_mul(apt4[it][:, :], psA[5][:, :], dcyl[vi][:, :]),
                         reads=[PK[5], f"dcyl{vi}"], writes=[f"apt4{it}"])
                    S.op("gpsimd", lambda E, n=n, hg=hg, it=it, vi=vi: E.tensor_tensor(
                        out=v34(xcm4[it][:, :]), in0=rep4(vldm[vi][:, :]), in1=bc4(dtT, n, hg), op=ALU.mult),
                        reads=[f"vldm{vi}", "dtT"], writes=[f"xcm4{it}"])
                    S.op("gpsimd", lambda E, n=n, hg=hg, it=it: E.tensor_tensor(
                        out=v34(kdm4[it][:, :]), in0=rep4(ktrm[it][:, :]), in1=bc4(edecT, n, hg), op=ALU.mult),
                        reads=[f"ktrm{it}", "edecT"], writes=[f"kdm4{it}"])
                    S.op("gpsimd", lambda E, n=n, hg=hg, p=p: E.tensor_tensor(
                        out=v34(Sinsc4[:, :]), in0=v34(SinF[p][:, :]), in1=bc4(cumPB, n, hg), op=ALU.mult),
                        reads=[f"SinF{p}", "cumPB"], writes=["Sinsc4"])
                    S.op("scalar", lambda E, Sprev=Sprev: E.copy(out=Sbf4[:, :], in_=Sprev[:, :]),
                         reads=[spk], writes=["Sbf4"])

                  def _stepB(hg=hg, n=n):
                    p = hg % 2
                    qg = qg2[p]
                    qk = f"qg{p}"
                    Sprev, Snext = (SA, SB_) if n % 2 == 0 else (SB_, SA)
                    spk, snk = ("SA", "SB_") if n % 2 == 0 else ("SB_", "SA")
                    kv = hg
                    it = n % 2
                    vi = n % 3
                    S.mms([(lambda E, hl=hl, it=it: E.matmul(
                        psA[6][:, hl * 128:(hl + 1) * 128], apt4[it][:, hl * 128:(hl + 1) * 128],
                        xcm4[it][:, hl * 128:(hl + 1) * 128], start=True, stop=True)) for hl in range(HG)],
                        reads=[f"apt4{it}", f"xcm4{it}"], writes=[PK[6]])
                    fns = []
                    for hl in range(HG):
                        fns.append(lambda E, hl=hl, n=n, qg=qg: E.matmul(
                            psA[4][:, hl * 128:(hl + 1) * 128],
                            qg[:, hl * TPC + n * 128: hl * TPC + (n + 1) * 128],
                            Sbf4[:, hl * 128:(hl + 1) * 128], start=True, stop=False))
                        fns.append(lambda E, hl=hl, n=n, qg=qg: E.matmul(
                            psA[4][:, hl * 128:(hl + 1) * 128],
                            qg[:, hl * TPC + n * 128: hl * TPC + (n + 1) * 128],
                            Sinsc4[:, hl * 128:(hl + 1) * 128], start=False, stop=True))
                    S.mms(fns, reads=[qk, "Sbf4", "Sinsc4"], writes=[PK[4]])
                    S.op("vector", lambda E, n=n, hg=hg: E.tensor_tensor(
                        out=v34(s1b4[:, :]), in0=v34(psA[4][:, :]), in1=bc4(expGT, n, hg), op=ALU.mult),
                        reads=[PK[4], "expGT"], writes=["s1b4"])
                    S.op("vector", lambda E: E.tensor_add(ytk4[:, :], s1b4[:, :], psA[6][:, :]),
                         reads=["s1b4", PK[6]], writes=["ytk40"])
                    S.mms([(lambda E, hl=hl, it=it: E.matmul(
                        psA[6][:, hl * 128:(hl + 1) * 128], kdm4[it][:, hl * 128:(hl + 1) * 128],
                        xcm4[it][:, hl * 128:(hl + 1) * 128], start=True, stop=True)) for hl in range(HG)],
                        reads=[f"kdm4{it}", f"xcm4{it}"], writes=[PK[6]])
                    for hl in range(HG):
                        S.op("scalar", lambda E, hl=hl: E.activation(
                            out=sdump[:, :], in_=ytk4[:, hl * 128:(hl + 1) * 128], func=AF.Square,
                            accum_out=rcol4[:, hl:hl + 1]),
                            reads=["ytk40"], writes=["sdump", "rsum"])
                    S.op("scalar", lambda E: E.activation(out=rcol4[:, 4:8], in_=rcol4[:, 0:4], func=AF.Sqrt,
                                                          scale=1.0 / HD, bias=float(RMS_EPS)),
                         reads=["rsum", "cpool"], writes=["rms4"])
                    S.op("vector", lambda E: E.reciprocal(rcol4[:, 8:12], rcol4[:, 4:8]),
                         reads=["rms4"], writes=["rstd4"])
                    S.op("vector", lambda E: E.tensor_tensor(
                        out=v34(Stmp[:, :]), in0=v34(ytk4[:, :]),
                        in1=rcol4[:, 8:12].rearrange("p (h o) -> p h o", o=1).broadcast_to([128, 4, 128]),
                        op=ALU.mult),
                        reads=["ytk40", "rstd4"], writes=["Stmp"])
                    S.op("vector", lambda E, it=it, vi=vi: E.tensor_mul(
                        ygb4x[it][:, :], Stmp[:, :], sgl[vi][:, :]),
                        reads=["Stmp", f"sgl{vi}"], writes=[f"ygb4{it}"])
                    S.op("vector", lambda E, n=n, hg=hg, Sprev=Sprev: E.tensor_tensor(
                        out=v34(s1b4[:, :]), in0=v34(Sprev[:, :]), in1=bc4(dcolsB, n, hg), op=ALU.mult),
                        reads=[spk, "dcolsB"], writes=["s1b4"])
                    S.op("vector", lambda E, Snext=Snext: E.tensor_add(Snext[:, :], s1b4[:, :], psA[6][:, :]),
                         reads=["s1b4", PK[6]], writes=[snk])
                  def _stepC(hg=hg, n=n):
                    it = n % 2
                    if True:
                        S.mms([(lambda E, hl=hl: E.transpose(
                            psT[:, hl * 128:(hl + 1) * 128], ygb4x[it][:, hl * 128:(hl + 1) * 128],
                            identb[:, :])) for hl in range(HG)],
                            reads=[f"ygb4{it}", "identb"], writes=["psT"])
                        S.op("scalar", lambda E: E.copy(out=ygs4[0][:, :], in_=psT[:, 0:512]),
                             reads=["psT"], writes=["ygs40"])
                        S.dma("sync", lambda E, hg=hg, n=n: E.dma_start(
                            out=ygT_dram[hg * 512:(hg + 1) * 512, n * 128:(n + 1) * 128]
                            .rearrange("(hl p) t -> p hl t", p=128),
                            in_=ygs4[0][:, :].rearrange("p (hl t) -> p hl t", hl=4)),
                            reads=["ygs40"], nodep=["ygd"], stream="ygd")
                  return [_stepA, _stepB, _stepC]

                for hg in range(NHG):
                    units = proj_units(hg)
                    steps = []
                    if hg > 0:
                        for n in range(NCH):
                            steps.extend(mixer_iter_steps(hg - 1, n))
                    nu, ns = len(units), len(steps)
                    si = 0
                    for j, u in enumerate(units):
                        u()
                        while si < ns and si < (j + 1) * ns // nu:
                            steps[si]()
                            si += 1
                    while si < ns:
                        steps[si]()
                        si += 1
                for n in range(NCH):
                    for st_ in mixer_iter_steps(NHG - 1, n):
                        st_()
                S.flush(block)
        if STAGE <= 4:
            S.stop = True

        S.barrier()
        with contextlib.ExitStack() as L1:
            wtile = [L1.enter_context(nc.sbuf_tensor(f"wo{i}", [128, 2048], BF16)) for i in range(3)]
            xrl = [L1.enter_context(nc.sbuf_tensor(f"xrl{i}", [128, 2, 512], F32)) for i in range(4)]
            x2st = [L1.enter_context(nc.sbuf_tensor(f"x2st{i}", [128, 512], F32)) for i in range(2)]
            for tc in range(NCH):
                if tc == 0:
                    S.dma("sync", lambda E, tc=tc: E.dma_start(
                        out=RB[:, tc * 4096:(tc + 1) * 4096].rearrange("p (m o) -> p m o", m=32),
                        in_=ygT_dram[:, tc * 128:(tc + 1) * 128].rearrange("(m p) o -> p m o", p=128)),
                        reads=["ygd"], writes=["p1big"], nodep=["ygres"], stream="ygres")
                else:
                    S.dma("sync", lambda E, tc=tc: E.dma_start(
                        out=RB[:, tc * 4096:(tc + 1) * 4096].rearrange("p (m o) -> p m o", m=32),
                        in_=ygT_dram[:, tc * 128:(tc + 1) * 128].rearrange("(m p) o -> p m o", p=128)),
                        reads=["ygd"], nodep=["ygres"], stream="ygres")
            psd = [psA[i][:, :] for i in range(7)] + [psT[:, :].bitcast(F32)]
            pkd = PK + ["psT"]

            def ld_wo(og, m4):
                bi = (og * 8 + m4) % 3
                S.dma("gpsimd", lambda E, m4=m4, og=og, bi=bi: E.dma_start(
                    out=wtile[bi][:, :],
                    in_=WoT[(og * 8 + m4) * 128:(og * 8 + m4 + 1) * 128, :]),
                    writes=[f"wo{bi}"])

            flatw = [(og, m4) for og in range(8) for m4 in range(NI // 4)]
            for j in range(2):
                ld_wo(*flatw[j])
            for idx, (og, m4) in enumerate(flatw):
                if idx + 2 < len(flatw):
                    ld_wo(*flatw[idx + 2])
                bi = (og * 8 + m4) % 3
                for mm in range(4):
                    m = m4 * 4 + mm
                    S.mms([(lambda E, m=m, mm=mm, t=t, bi=bi: E.matmul(
                        psd[t], RB[:, t * 4096 + m * 128: t * 4096 + (m + 1) * 128],
                        wtile[bi][:, mm * 512:(mm + 1) * 512], start=(m == 0), stop=(m == NI - 1)))
                        for t in range(8)],
                        reads=[f"wo{bi}", "ygres"],
                        writes=pkd)
                if m4 == 5:
                    for t in range(8):
                        S.dma("sync", lambda E, t=t, og=og: E.dma_start(
                            out=xrl[t % 4][:, t // 4, :], in_=x[t * 128:(t + 1) * 128, og * 512:(og + 1) * 512]),
                            writes=[f"xrl{t % 4}"])
                if m4 == NI // 4 - 1:
                    for t in range(8):
                        bo = t % 2
                        S.op("vector", lambda E, t=t, bo=bo: E.tensor_add(
                            x2st[bo][:, :], psd[t], xrl[t % 4][:, t // 4, :]),
                            reads=[pkd[t], f"xrl{t % 4}"], writes=[f"x2st{bo}"])
                        S.dma("sync", lambda E, t=t, og=og, bo=bo: E.dma_start(
                            out=x2_dram[t * 128:(t + 1) * 128, og * 512:(og + 1) * 512], in_=x2st[bo][:, :]),
                            reads=[f"x2st{bo}"], nodep=["x2d"])
            S.flush(block)

        S.barrier()
        with contextlib.ExitStack() as L1:
            xt = [L1.enter_context(nc.sbuf_tensor(f"x2t{i}", [128, DM], F32)) for i in range(2)]
            xnt = L1.enter_context(nc.sbuf_tensor("xn2t", [128, DM], BF16))
            cols = L1.enter_context(nc.sbuf_tensor("cols2", [128, 8], F32))
            S.wr["p1big"] = S.wr.get("ygres")
            ln_transpose(x2_dram, "x2d", xt, xnt, cols, "xn2big", "p2")
            S.flush(block)
        if STAGE <= 5:
            S.stop = True

        S.barrier()
        with contextlib.ExitStack() as L1:
            wga = [L1.enter_context(nc.sbuf_tensor(f"wga{i}", [128, 32 * 128], BF16)) for i in range(2)]
            wup = [L1.enter_context(nc.sbuf_tensor(f"wup{i}", [128, 32 * 128], BF16)) for i in range(2)]
            gsb = L1.enter_context(nc.sbuf_tensor("gsb", [128, 1024], BF16))
            usb = L1.enter_context(nc.sbuf_tensor("usb", [128, 1024], BF16))
            hsb = [L1.enter_context(nc.sbuf_tensor(f"hsb{i}", [128, 1024], BF16)) for i in range(2)]
            sgc = L1.enter_context(nc.sbuf_tensor("sgc", [128, NKI], F32))
            suc = L1.enter_context(nc.sbuf_tensor("suc", [128, NKI], F32))
            S.dma("sync", lambda E: E.dma_start(out=sgc[:, :], in_=sgate2[:, :]), writes=["sgc"], stream="CONST")
            S.dma("sync", lambda E: E.dma_start(out=suc[:, :], in_=sup2[:, :]), writes=["suc"], stream="CONST")

            for oc in range(NKI):
                bi = oc % 2
                S.dma("gpsimd", lambda E, oc=oc, bi=bi: E.dma_start(
                    out=wga[bi][:, :], in_=WgaT[oc * 128:(oc + 1) * 128, :]),
                    writes=[f"wga{bi}"])
                S.dma("gpsimd", lambda E, oc=oc, bi=bi: E.dma_start(
                    out=wup[bi][:, :], in_=WupT[oc * 128:(oc + 1) * 128, :]),
                    writes=[f"wup{bi}"])
                S.mms([(lambda E, i=i, th=th, bi=bi: E.matmul(
                    psA[th][:, :], wga[bi][:, i * 128:(i + 1) * 128],
                    RB[:, i * TPC + th * 512: i * TPC + th * 512 + 512],
                    start=(i == 0), stop=(i == NI - 1))) for i in range(NI) for th in range(2)],
                    reads=[f"wga{bi}", "xn2big"], writes=[PK[0], PK[1]])
                S.mms([(lambda E, i=i, th=th, bi=bi: E.matmul(
                    psA[2 + th][:, :], wup[bi][:, i * 128:(i + 1) * 128],
                    RB[:, i * TPC + th * 512: i * TPC + th * 512 + 512],
                    start=(i == 0), stop=(i == NI - 1))) for i in range(NI) for th in range(2)],
                    reads=[f"wup{bi}", "xn2big"], writes=[PK[2], PK[3]])
                for th in range(2):
                    S.op("scalar", lambda E, th=th, oc=oc: E.activation(
                        out=gsb[:, th * 512:th * 512 + 512], in_=psA[th][:, :], func=AF.Silu,
                        scale=sgc[:, oc:oc + 1]),
                        reads=[PK[th], "sgc"], writes=["gsb"])
                    S.op("vector", lambda E, th=th, oc=oc: E.tensor_scalar_mul(
                        usb[:, th * 512:th * 512 + 512], psA[2 + th][:, :], suc[:, oc:oc + 1]),
                        reads=[PK[2 + th], "suc"], writes=["usb"])
                hb = oc % 2
                S.op("vector", lambda E, hb=hb: E.tensor_mul(hsb[hb][:, :], gsb[:, :], usb[:, :]),
                     reads=["gsb", "usb"], writes=[f"hsb{hb}"])
                S.dma("sync", lambda E, oc=oc, hb=hb: E.dma_start(
                    out=h_dram[oc * 128:(oc + 1) * 128, :], in_=hsb[hb][:, :]),
                    reads=[f"hsb{hb}"], nodep=["hd"])
            S.flush(block)
        if STAGE <= 6:
            S.stop = True

        S.barrier()
        with contextlib.ExitStack() as L1:
            hbf = [L1.enter_context(nc.sbuf_tensor(f"hbf{i}", [128, 1024], BF16)) for i in range(4)]
            wdn = [L1.enter_context(nc.sbuf_tensor(f"wdn{i}", [128, 1024], BF16)) for i in range(3)]
            sdb = L1.enter_context(nc.sbuf_tensor("sdb", [128, DM], F32))
            x2r = [L1.enter_context(nc.sbuf_tensor(f"x2r{i}", [128, 2, 512], F32)) for i in range(4)]
            ost = [L1.enter_context(nc.sbuf_tensor(f"ost{i}", [128, 512], F32)) for i in range(2)]
            S.dma("sync", lambda E: E.dma_start(out=sdb[:, :], in_=sdownb[:, :]), writes=["sdb"], stream="CONST")

            psd = [psA[i][:, :] for i in range(7)] + [psT[:, :].bitcast(F32)]
            pkd = PK + ["psT"]

            def ld_hbf(og, k):
                bi = (og * NKI + k) % 4
                S.dma("sync", lambda E, k=k, bi=bi: E.dma_start(
                    out=hbf[bi][:, :], in_=h_dram[k * 128:(k + 1) * 128, :]),
                    reads=["hd"], writes=[f"hbf{bi}"])

            def ld_wdn(og, k2):
                wi = (og * 43 + k2) % 3
                S.dma("gpsimd", lambda E, k2=k2, og=og, wi=wi: E.dma_start(
                    out=wdn[wi][:, :],
                    in_=WdnT[(og * 43 + k2) * 128:(og * 43 + k2 + 1) * 128, :]),
                    writes=[f"wdn{wi}"])

            flat = [(og, k) for og in range(8) for k in range(NKI)]
            flat2 = [(og, k2) for og in range(8) for k2 in range(43)]
            LOOK = 3
            for j in range(LOOK):
                ld_hbf(*flat[j])
            for j in range(2):
                ld_wdn(*flat2[j])
            w2i = 2
            for idx, (og, k) in enumerate(flat):
                if idx + LOOK < len(flat):
                    ld_hbf(*flat[idx + LOOK])
                if k % 2 == 0 and w2i < len(flat2):
                    ld_wdn(*flat2[w2i])
                    w2i += 1
                bi = (og * NKI + k) % 4
                wi = (og * 43 + k // 2) % 3
                if k == 0:
                    for t in range(8):
                        S.mms([(lambda E, t=t, bi=bi, wi=wi: E.matmul(
                            psd[t], hbf[bi][:, t * 128:(t + 1) * 128],
                            wdn[wi][:, 0:512], start=True, stop=False))],
                            reads=[f"hbf{bi}", f"wdn{wi}"], writes=[pkd[t]])
                else:
                    S.mms([(lambda E, k=k, t=t, bi=bi, wi=wi: E.matmul(
                        psd[t], hbf[bi][:, t * 128:(t + 1) * 128],
                        wdn[wi][:, (k % 2) * 512:(k % 2 + 1) * 512], start=False, stop=(k == NKI - 1)))
                        for t in range(8)],
                        reads=[f"hbf{bi}", f"wdn{wi}"],
                        writes=pkd)
                if k == 40:
                    for t in range(8):
                        S.dma("sync", lambda E, t=t, og=og: E.dma_start(
                            out=x2r[t % 4][:, t // 4, :], in_=x2_dram[t * 128:(t + 1) * 128, og * 512:(og + 1) * 512]),
                            reads=["x2d"], writes=[f"x2r{t % 4}"])
                if k == NKI - 1:
                    for t in range(8):
                        bo = t % 2
                        S.op("vector", lambda E, t=t, og=og, bo=bo: E.tensor_mul(
                            ost[bo][:, :], psd[t], sdb[:, og * 512:(og + 1) * 512]),
                            reads=[pkd[t], "sdb"], writes=[f"ost{bo}"])
                        S.op("vector", lambda E, t=t, bo=bo: E.tensor_add(
                            ost[bo][:, :], ost[bo][:, :], x2r[t % 4][:, t // 4, :]),
                            reads=[f"ost{bo}", f"x2r{t % 4}"], writes=[f"ost{bo}"])
                        S.dma("sync", lambda E, t=t, og=og, bo=bo: E.dma_start(
                            out=out[t * 128:(t + 1) * 128, og * 512:(og + 1) * 512], in_=ost[bo][:, :]),
                            reads=[f"ost{bo}"], nodep=["outd"])
            if not S.stop:
                ent = S.dstream["outd@sync"]
                S.raw("sync", lambda E, ent=ent: E.wait_ge(ent[0], ent[1]))
            S.flush(block)

    return nc


def _rope_tables(t0):
    inv_freq = 1.0 / 10000.0 ** (np.arange(0, HD, 2, dtype=np.float64) / HD)
    ang = np.arange(t0, t0 + TPC, dtype=np.float64)[:, None] * inv_freq
    cos = np.concatenate([np.cos(ang), np.cos(ang)], axis=-1)
    sin = np.concatenate([np.sin(ang), np.sin(ang)], axis=-1)
    return (np.ascontiguousarray(cos.T.astype(np.float32)),
            np.ascontiguousarray(sin.T.astype(np.float32)))


def kernel(L0_attn_hidden_states, L0_attn_Wq, L0_attn_Wk, L0_attn_Wv, L0_attn_Wo,
           L0_attn_Wg, L0_attn_Win, L0_attn_dt_bias, L0_attn_A_log_bias, L0_attn_gnorm_w,
           L0_ffn_w_gate, L0_ffn_s_gate, L0_ffn_w_up, L0_ffn_s_up, L0_ffn_w_down, L0_ffn_s_down):
    x = np.asarray(L0_attn_hidden_states, np.float32)
    shared = {
        "WqT": np.ascontiguousarray(np.asarray(L0_attn_Wq, np.float32)
                                    .reshape(32, 128, 32, 128).transpose(0, 3, 2, 1).reshape(DM, DM)),
        "WkT": np.ascontiguousarray(np.asarray(L0_attn_Wk, np.float32)
                                    .reshape(8, 128, 32, 128).transpose(0, 3, 2, 1).reshape(KVH * HD, DM)),
        "WvT": np.ascontiguousarray(np.asarray(L0_attn_Wv, np.float32).T
                                    .reshape(8, 4, 128, 2, 512).transpose(3, 0, 2, 1, 4).reshape(2048, 2048)),
        "WgT": np.ascontiguousarray(np.asarray(L0_attn_Wg, np.float32).T
                                    .reshape(8, 4, 128, 8, 512).transpose(3, 0, 2, 1, 4).reshape(8192, 2048)),
        "WoT": np.ascontiguousarray(np.asarray(L0_attn_Wo, np.float32).T
                                    .reshape(8, 4, 128, 8, 512).transpose(3, 0, 2, 1, 4).reshape(8192, 2048)),
        "WinT": np.ascontiguousarray(np.asarray(L0_attn_Win, np.float32)
                                     .reshape(H, 32, 128).transpose(2, 1, 0).reshape(128, 32 * H)),
        "WgaT": np.ascontiguousarray(np.asarray(L0_ffn_w_gate, np.int32)
                                     .reshape(NKI, 128, 32, 128).transpose(0, 3, 2, 1).reshape(INTER, DM)),
        "WupT": np.ascontiguousarray(np.asarray(L0_ffn_w_up, np.int32)
                                     .reshape(NKI, 128, 32, 128).transpose(0, 3, 2, 1).reshape(INTER, DM)),
        "WdnT": np.ascontiguousarray(np.asarray(L0_ffn_w_down, np.int32).T
                                     .reshape(43, 2, 128, 8, 512).transpose(3, 0, 2, 1, 4).reshape(44032, 1024)),
        "sgate2": np.ascontiguousarray(np.asarray(L0_ffn_s_gate, np.float32).reshape(NKI, 128).T),
        "sup2": np.ascontiguousarray(np.asarray(L0_ffn_s_up, np.float32).reshape(NKI, 128).T),
        "sdownb": np.ascontiguousarray(np.broadcast_to(np.asarray(L0_ffn_s_down, np.float32), (128, DM))),
        "gnormr": np.ascontiguousarray(np.broadcast_to(
            np.tile(np.asarray(L0_attn_gnorm_w, np.float32), 4), (128, 512))),
        "dtb": np.ascontiguousarray(np.asarray(L0_attn_dt_bias, np.float32).reshape(H, 1)),
        "Alb": np.ascontiguousarray(np.asarray(L0_attn_A_log_bias, np.float32).reshape(H, 1)),
        "identf_h": np.eye(128, dtype=np.float32),
        "ones32_h": np.ones((32, CH), np.float32),
        "trig8_h": np.triu(np.ones((8, 8), np.float32), 1).T.copy(),
        "cpool_h": np.ascontiguousarray(np.broadcast_to(np.array(CVALS, np.float32), (128, 4))),
    }
    mb = np.full((128, 128), -1.0e6, np.float32)
    iu = np.arange(128)
    mb[iu[:, None] <= iu[None, :]] = 0.0
    shared["maskbig4_h"] = np.ascontiguousarray(np.tile(mb, (1, 4)))
    oh4 = np.zeros((NHG * 32, 512), np.float32)
    for hg in range(NHG):
        for hl in range(HG):
            oh4[hg * 32 + hg * 4 + hl, hl * 128:(hl + 1) * 128] = 1.0
    shared["oh4_h"] = oh4
    oh8 = np.zeros((8, 8 * 128), np.float32)
    for r in range(8):
        oh8[r, r * 128:(r + 1) * 128] = 1.0
    shared["oh8_h"] = oh8

    in_maps = []
    for c in range(NCORES):
        b, ci = c // 4, c % 4
        t0 = ci * TPC
        cos, sin = _rope_tables(t0)
        m = np.zeros((NCORES, H), np.float32)
        for r in range(NCORES):
            if r // 4 == b and r < c:
                m[r, :] = 1.0
        d = dict(shared)
        d["x"] = np.ascontiguousarray(x[b, t0:t0 + TPC])
        d["cosr"] = cos
        d["sinr"] = sin
        d["cmask"] = m
        in_maps.append(d)

    nc = build_nc()
    res = run_bass_kernel_spmd(nc, in_maps, core_ids=list(range(NCORES)))
    if DBG:
        kernel._last = res
    outs = [np.asarray(res.results[c]["out"], np.float32) for c in range(NCORES)]
    full = np.empty((B, L, DM), np.float32)
    for c in range(NCORES):
        b, ci = c // 4, c % 4
        full[b, ci * TPC:(ci + 1) * TPC] = outs[c]
    return full


# version 16
# speedup vs baseline: 1.0874x; 1.0874x over previous
import os
import contextlib
import numpy as np

import concourse.bass as bass
import concourse.mybir as mybir
from concourse.bass_utils import run_bass_kernel_spmd

F32 = mybir.dt.float32
BF16 = mybir.dt.bfloat16
I32 = mybir.dt.int32
AF = mybir.ActivationFunctionType
ALU = mybir.AluOpType

DBG = os.environ.get("MK_DEBUG", "") != ""
STAGE = int(os.environ.get("MK_STAGE", "9"))
NOAG = os.environ.get("MK_NOAG", "") != ""

B, L, DM = 2, 4096, 4096
NCORES = 8
TPC = 1024
CH = 128
NCH = TPC // CH
H, KVH, HD = 32, 8, 128
GQ = H // KVH
NI = DM // 128
INTER = 11008
NKI = INTER // 128
HG = 4
NHG = H // HG
RMS_EPS = 1e-5
LN_EPS = 1e-5
AGR = H * HD + 1
CVALS = (1e-5, 1.0, -1.0, 1.0 / 128.0)

ENGS = ("tensor", "vector", "scalar", "gpsimd", "sync")
DQS = ("gpsimd", "sync", "scalar")


class Sched:

    BATCH = {"CONST", "gt8", "ygd", "x2d", "hd", "ag_in", "outd", "ygres", "dbg", "vtd", "sgd", "dcyd"}

    def __init__(self, nc, stack):
        self.nc = nc
        self.esem = {e: stack.enter_context(nc.semaphore("Se" + e)) for e in ENGS}
        self.agsem = stack.enter_context(nc.semaphore("Sag"))
        self.dpool = [stack.enter_context(nc.semaphore(f"Sm{i}")) for i in range(70)]
        self.dstream = {}
        self.ecnt = {e: 0 for e in ENGS}
        self.agcnt = 0
        self.wr = {}
        self.rd = {}
        self.seen = {e: {} for e in ENGS}
        self.pend = {e: [] for e in ENGS}
        self.stop = False

    def _stream(self, s):
        if s not in self.dstream:
            assert len(self.dstream) < len(self.dpool), "DMA sem pool exhausted"
            self.dstream[s] = [self.dpool[len(self.dstream)], 0]
        return self.dstream[s]

    def _semof(self, tok):
        kind = tok[0]
        if kind == "e":
            return "e" + tok[1], self.esem[tok[1]], tok[2]
        if kind == "m":
            s = tok[1]
            ent = self.dstream[s]
            val = ent[1] if s.split("@")[0] in self.BATCH else tok[2]
            return "m" + s, ent[0], val
        return "ag", self.agsem, tok[1]

    def _waits(self, eng, reads, writes):
        need = {}
        if getattr(self, "_barrier_pending", None) and self._barrier_pending.get(eng):
            self._barrier_pending[eng] = False
            for tok in self._barrier:
                if tok[0] == "e" and tok[1] == eng:
                    continue
                key, sem, cnt = self._semof(tok)
                if self.seen[eng].get(key, 0) < cnt:
                    need[key] = (sem, cnt)
                    self.seen[eng][key] = cnt

        def add(tok):
            if tok is None:
                return
            if tok[0] == "e" and tok[1] == eng and eng == "tensor":
                return
            key, sem, cnt = self._semof(tok)
            if self.seen[eng].get(key, 0) >= cnt:
                return
            if key not in need or need[key][1] < cnt:
                need[key] = (sem, cnt)

        for k in reads:
            add(self.wr.get(k))
        for k in writes:
            add(self.wr.get(k))
            for t in self.rd.get(k, ()):
                add(t)
        for key, (_s, c) in need.items():
            self.seen[eng][key] = max(self.seen[eng].get(key, 0), c)
        return list(need.values())

    def _record(self, reads, writes, tok):
        for k in reads:
            self.rd.setdefault(k, []).append(tok)
        for k in writes:
            self.wr[k] = tok
            self.rd[k] = []

    def op(self, eng, fn, reads=(), writes=()):
        if self.stop:
            return None
        waits = self._waits(eng, reads, writes)
        self.ecnt[eng] += 1
        tok = ("e", eng, self.ecnt[eng])
        self.pend[eng].append((waits, [fn], self.esem[eng], 1))
        self._record(reads, writes, tok)
        return tok

    def mms(self, fns, reads=(), writes=()):
        if self.stop:
            return None
        waits = self._waits("tensor", reads, writes)
        self.ecnt["tensor"] += 1
        tok = ("e", "tensor", self.ecnt["tensor"])
        self.pend["tensor"].append((waits, list(fns), self.esem["tensor"], 1))
        self._record(reads, writes, tok)
        return tok

    def dma(self, q, fn, reads=(), writes=(), stream=None, nodep=()):
        if self.stop:
            return None
        s = (stream if stream is not None else (list(writes) + list(nodep))[0]) + "@" + q
        ent = self._stream(s)
        waits = self._waits(q, reads, writes)
        ent[1] += 16
        tok = ("m", s, ent[1])
        self.pend[q].append((waits, [fn], ent[0], 16))
        self._record(reads, list(writes) + list(nodep), tok)
        return tok

    def collective(self, fn, reads=(), writes=()):
        if self.stop:
            return None
        waits = self._waits("gpsimd", reads, writes)
        self.agcnt += 1
        tok = ("ag", self.agcnt)
        self.pend["gpsimd"].append((waits, [fn], self.agsem, None))
        self._record(reads, writes, tok)
        return tok

    def raw(self, eng, fn):
        if self.stop:
            return
        self.pend[eng].append(([], [fn], None, 0))

    def barrier(self):
        if self.stop:
            return
        toks = []
        for e in ENGS:
            if self.ecnt[e]:
                toks.append(("e", e, self.ecnt[e]))
        for s, ent in self.dstream.items():
            if ent[1]:
                toks.append(("m", s, ent[1]))
        self._barrier = toks
        self._barrier_pending = {e: True for e in ENGS}

    def flush(self, block):
        for eng in ENGS:
            ops = self.pend[eng]
            if not ops:
                continue
            self.pend[eng] = []

            def body(E, ops=ops):
                for waits, fns, sem, inc in ops:
                    for s, c in waits:
                        E.wait_ge(s, c)
                    inst = None
                    for f in fns:
                        inst = f(E)
                    if sem is not None:
                        if inc is None:
                            inst.then_inc(sem)
                        elif inc:
                            inst.then_inc(sem, inc)
            getattr(block, eng)(body)


def build_nc():
    nc = bass.Bass("TRN2", target_bir_lowering=False)

    def inp(name, shape, dtype=F32):
        return nc.dram_tensor(name, list(shape), dtype, kind="ExternalInput").ap()

    x = inp("x", (TPC, DM))
    WqT = inp("WqT", (DM, DM))
    WkT = inp("WkT", (KVH * HD, DM))
    WvT = inp("WvT", (2048, 2048))
    WgT = inp("WgT", (8192, 2048))
    WoT = inp("WoT", (8192, 2048))
    WinT = inp("WinT", (128, 32 * H))
    WgaT = inp("WgaT", (INTER, DM), I32)
    WupT = inp("WupT", (INTER, DM), I32)
    WdnT = inp("WdnT", (44032, 1024), I32)
    sgate2 = inp("sgate2", (128, NKI))
    sup2 = inp("sup2", (128, NKI))
    sdownb = inp("sdownb", (128, DM))
    gnormr = inp("gnormr", (128, 512))
    dtb = inp("dtb", (H, 1))
    Alb = inp("Alb", (H, 1))
    cosr = inp("cosr", (128, TPC))
    sinr = inp("sinr", (128, TPC))
    cmask = inp("cmask", (NCORES, H))
    identf_h = inp("identf_h", (128, 128))
    maskbig4_h = inp("maskbig4_h", (128, 512))
    ones32_h = inp("ones32_h", (32, CH))
    oh4_h = inp("oh4_h", (NHG * 32, 512))
    oh8_h = inp("oh8_h", (8, 8 * 128))
    trig8_h = inp("trig8_h", (8, 8))
    cpool_h = inp("cpool_h", (128, 4))

    out = nc.dram_tensor("out", [TPC, DM], F32, kind="ExternalOutput").ap()

    dkind = "ExternalOutput" if DBG else "Internal"
    x2_dram = nc.dram_tensor("x2_dram", [TPC, DM], F32, kind=dkind).ap()
    ygT_dram = nc.dram_tensor("ygT_dram", [DM, TPC], BF16, kind=dkind).ap()
    h_dram = nc.dram_tensor("h_dram", [INTER, TPC], BF16, kind="Internal").ap()
    vt_dram = nc.dram_tensor("vt_dram", [128, NCH * KVH * HD], BF16, kind="Internal").ap()
    sg_dram = nc.dram_tensor("sg_dram", [128, NHG * NCH * 512], BF16, kind="Internal").ap()
    dcy_dram = nc.dram_tensor("dcy_dram", [128, NHG * NCH * 512], BF16, kind="Internal").ap()
    ag_in_t = nc.dram_tensor("ag_in", [AGR, HD], F32, kind="Internal")
    ag_out_t = nc.dram_tensor("ag_out", [NCORES * AGR, HD], F32, kind="Internal")
    ag_in = ag_in_t.ap()
    ag_out = ag_out_t.ap()
    if DBG:
        dbg_xn = nc.dram_tensor("dbg_xn", [TPC, DM], F32, kind="ExternalOutput").ap()
        dbg_kfm = nc.dram_tensor("dbg_kfm", [128, KVH * TPC], F32, kind="ExternalOutput").ap()
        dbg_vtok = nc.dram_tensor("dbg_vtok", [128, NCH * KVH * HD], F32, kind="ExternalOutput").ap()
        dbg_GT = nc.dram_tensor("dbg_GT", [128, NCH * H], F32, kind="ExternalOutput").ap()
        dbg_dtT = nc.dram_tensor("dbg_dtT", [128, NCH * H], F32, kind="ExternalOutput").ap()
        dbg_q = nc.dram_tensor("dbg_q", [NHG * 128, HG * TPC], F32, kind="ExternalOutput").ap()
        dbg_sg = nc.dram_tensor("dbg_sg", [128, NHG * NCH * 512], F32, kind="ExternalOutput").ap()
        dbg_Sin = nc.dram_tensor("dbg_Sin", [NHG * 128, HG * HD], F32, kind="ExternalOutput").ap()

    with contextlib.ExitStack() as st:
        S = Sched(nc, st)

        def sb(name, shape, dtype):
            return st.enter_context(nc.sbuf_tensor(name, list(shape), dtype))

        identf = sb("identf", (128, 128), F32)
        identb = sb("identb", (128, 128), BF16)
        maskbig4 = sb("maskbig4", (128, 512), F32)
        ones32 = sb("ones32", (32, CH), F32)
        trig8 = sb("trig8", (8, 8), F32)
        m8 = sb("m8", (8, H), F32)
        dtbc = sb("dtbc", (H, 1), F32)
        Albc = sb("Albc", (H, 1), F32)
        RB = sb("RB", (128, NI * TPC), BF16)
        cpool = sb("cpool", (128, 4), F32)
        psA = [st.enter_context(nc.psum_tensor(f"psA{i}", [128, 512], F32)) for i in range(7)]
        psT = st.enter_context(nc.psum_tensor("psT", [128, 1024], BF16))
        PK = [f"psA{i}" for i in range(7)]

        def rep4(tile):
            return tile.rearrange("p (o e) -> p o e", o=1).broadcast_to([128, 4, 128])

        def bc4(tab, n, hg):
            c0 = n * 32 + 4 * hg
            return tab[:, c0:c0 + 4].rearrange("p (h o) -> p h o", o=1).broadcast_to([128, 4, 128])

        def v34(tile):
            return tile.rearrange("p (h e) -> p h e", h=4)

        block = st.enter_context(nc.Block())

        S.dma("sync", lambda E: E.dma_start(out=identf[:, :], in_=identf_h[:, :]), writes=["identf"], stream="CONST")
        S.dma("gpsimd", lambda E: E.dma_start(out=identb[:, :], in_=identf_h[:, :]), writes=["identb"], stream="CONST")
        S.dma("sync", lambda E: E.dma_start(out=maskbig4[:, :], in_=maskbig4_h[:, :]), writes=["maskbig"], stream="CONST")
        S.dma("sync", lambda E: E.dma_start(out=ones32[:, :], in_=ones32_h[:, :]), writes=["ones32"], stream="CONST")
        S.dma("sync", lambda E: E.dma_start(out=trig8[:, :], in_=trig8_h[:, :]), writes=["trig8"], stream="CONST")
        S.dma("sync", lambda E: E.dma_start(out=m8[:, :], in_=cmask[:, :]), writes=["m8"], stream="CONST")
        S.dma("sync", lambda E: E.dma_start(out=dtbc[:, :], in_=dtb[:, :]), writes=["dtbc"], stream="CONST")
        S.dma("sync", lambda E: E.dma_start(out=Albc[:, :], in_=Alb[:, :]), writes=["Albc"], stream="CONST")
        S.dma("sync", lambda E: E.dma_start(out=cpool[:, :], in_=cpool_h[:, :]), writes=["cpool"], stream="CONST")
        for j, v in enumerate(CVALS):
            nc.const_aps.register(F32, float(v), cpool[:, j:j + 1])

        def ln_transpose(src_dram, src_key, xt, xnt, cols, dst_key, kp):
            for tc in range(NCH):
                bi = tc % 2
                xk = f"{kp}xt{bi}"
                S.dma("sync", lambda E, tc=tc, bi=bi: E.dma_start(
                    out=xt[bi][:, :], in_=src_dram[tc * 128:(tc + 1) * 128, :]),
                    reads=[src_key] if src_key else [], writes=[xk])
                S.op("vector", lambda E, bi=bi: E.reduce_sum(cols[:, 0:1], xt[bi][:, :], axis=mybir.AxisListType.X),
                     reads=[xk], writes=[kp + "c0"])
                S.op("scalar", lambda E, bi=bi: E.activation(out=xnt[:, :], in_=xt[bi][:, :], func=AF.Square,
                                                             accum_out=cols[:, 1:2]),
                     reads=[xk], writes=["xnt" + kp, kp + "c1"])
                S.op("vector", lambda E: E.tensor_scalar_mul(cols[:, 2:3], cols[:, 0:1], 1.0 / DM),
                     reads=[kp + "c0"], writes=[kp + "mean"])
                S.op("scalar", lambda E: E.activation(out=cols[:, 3:4], in_=cols[:, 2:3], func=AF.Square),
                     reads=[kp + "mean"], writes=[kp + "m2"])
                S.op("vector", lambda E: E.tensor_scalar_mul(cols[:, 4:5], cols[:, 1:2], 1.0 / DM),
                     reads=[kp + "c1"], writes=[kp + "sq"])
                S.op("vector", lambda E: E.tensor_sub(cols[:, 5:6], cols[:, 4:5], cols[:, 3:4]),
                     reads=[kp + "sq", kp + "m2"], writes=[kp + "var"])
                S.op("scalar", lambda E: E.activation(out=cols[:, 6:7], in_=cols[:, 5:6], func=AF.Sqrt,
                                                      bias=float(LN_EPS)),
                     reads=[kp + "var", "cpool"], writes=[kp + "std"])
                S.op("vector", lambda E: E.reciprocal(cols[:, 7:8], cols[:, 6:7]),
                     reads=[kp + "std"], writes=[kp + "rstd"])
                S.op("vector", lambda E: E.tensor_scalar_mul(cols[:, 2:3], cols[:, 2:3], -1.0),
                     reads=[kp + "mean"], writes=[kp + "mean"])
                S.op("vector", lambda E, bi=bi: E.tensor_scalar(out=xnt[:, :], in0=xt[bi][:, :],
                                                                scalar1=cols[:, 2:3], scalar2=cols[:, 7:8],
                                                                op0=ALU.add, op1=ALU.mult),
                     reads=[xk, kp + "mean", kp + "rstd"], writes=["xnt" + kp])
                if DBG and kp == "p1":
                    S.dma("gpsimd", lambda E, tc=tc: E.dma_start(out=dbg_xn[tc * 128:(tc + 1) * 128, :], in_=xnt[:, :]),
                          reads=["xnt" + kp], stream="dbg", nodep=["dbg"])
                for i in range(NI):
                    pb = psT[:, 0:512] if i % 2 == 0 else psA[0][:, 0:256].bitcast(BF16)
                    pk2 = "psT" if i % 2 == 0 else "psA0"
                    S.op("tensor", lambda E, i=i, pb=pb: E.transpose(
                        pb[:, 0:128], xnt[:, i * 128:(i + 1) * 128], identb[:, :]),
                        reads=["xnt" + kp, "identb"], writes=[pk2])
                    S.op("scalar", lambda E, i=i, tc=tc, pb=pb: E.copy(
                        out=RB[:, i * TPC + tc * 128: i * TPC + (tc + 1) * 128],
                        in_=pb[:, 0:128]),
                        reads=[pk2], writes=[dst_key])

        with contextlib.ExitStack() as L1:
            def sb1(name, shape, dtype):
                return L1.enter_context(nc.sbuf_tensor(name, list(shape), dtype))

            cosb = sb1("cosb", (128, TPC), F32)
            sinb = sb1("sinb", (128, TPC), F32)
            kfm = sb1("kfm", (128, KVH * TPC), BF16)
            negGT = sb1("negGT", (128, NCH * H), F32)
            expGT = sb1("expGT", (128, NCH * H), F32)
            edecT = sb1("edecT", (128, NCH * H), F32)
            dcolsB = sb1("dcolsB", (128, NCH * H), F32)
            cumPB = sb1("cumPB", (128, NCH * H), F32)
            dtT = sb1("dtT", (128, NCH * H), F32)
            Gfm = sb1("Gfm", (32, TPC), F32)
            SA = sb1("SA", (128, HG * HD), F32)
            SB_ = sb1("SB_", (128, HG * HD), F32)
            Sin = sb1("Sin", (128, HG * HD), F32)
            Sin2 = sb1("Sin2", (128, HG * HD), F32)
            wblk = [sb1(f"wblk{i}", (128, 32 * 128), BF16) for i in range(2)]
            wtile = [sb1(f"wtile{i}", (128, 2048), BF16) for i in range(4)]
            oh8b = sb1("oh8b", (8, 8 * 128), F32)
            coefB = sb1("coefB", (128, NCORES * H), F32)
            gt8 = sb1("gt8", (8, H), F32)
            cf8 = sb1("cf8", (8, H), F32)

            S.dma("sync", lambda E: E.dma_start(out=cosb[:, :], in_=cosr[:, :]), writes=["cos"], stream="CONST")
            S.dma("sync", lambda E: E.dma_start(out=sinb[:, :], in_=sinr[:, :]), writes=["sin"], stream="CONST")
            S.dma("sync", lambda E: E.dma_start(out=oh8b[:, :], in_=oh8_h[:, :]), writes=["oh8b"], stream="CONST")

            with contextlib.ExitStack() as L2:
                xt = [L2.enter_context(nc.sbuf_tensor(f"xt{i}", [128, DM], F32)) for i in range(2)]
                xnt = L2.enter_context(nc.sbuf_tensor("xnt", [128, DM], BF16))
                cols = L2.enter_context(nc.sbuf_tensor("cols", [128, 8], F32))
                ln_transpose(x, None, xt, xnt, cols, "p1big", "p1")
                S.flush(block)
            if STAGE <= 1:
                S.stop = True

            S.barrier()
            with contextlib.ExitStack() as L2:
                zdt = L2.enter_context(nc.sbuf_tensor("zdt", [32, TPC], F32))
                gfm = L2.enter_context(nc.sbuf_tensor("gfm", [32, TPC], F32))
                dfm = L2.enter_context(nc.sbuf_tensor("dfm", [32, TPC], F32))
                rot = L2.enter_context(nc.sbuf_tensor("rot", [128, 512], F32))
                t1b = L2.enter_context(nc.sbuf_tensor("t1b", [128, 512], F32))
                t2b = L2.enter_context(nc.sbuf_tensor("t2b", [128, 512], F32))
                sm8 = L2.enter_context(nc.sbuf_tensor("sm8", [32, 64], F32))
                edsfT = L2.enter_context(nc.sbuf_tensor("edsfT", [128, NCH * H], F32))
                vst = [L2.enter_context(nc.sbuf_tensor(f"vst{i}", [128, 512], BF16)) for i in range(2)]
                vld = [L2.enter_context(nc.sbuf_tensor(f"vldA{i}", [128, 128], BF16)) for i in range(3)]
                ktr = [L2.enter_context(nc.sbuf_tensor(f"ktrA{i}", [128, 128], BF16)) for i in range(2)]
                sm8T = L2.enter_context(nc.sbuf_tensor("sm8T", [8, 128], F32))
                agst = L2.enter_context(nc.sbuf_tensor("agst", [128, 1024], F32))
                xc4 = [L2.enter_context(nc.sbuf_tensor(f"xc4{i}", [128, 512], BF16)) for i in range(2)]
                kd4 = [L2.enter_context(nc.sbuf_tensor(f"kd4{i}", [128, 512], BF16)) for i in range(2)]

                for kv in range(KVH):
                    S.dma("gpsimd", lambda E, kv=kv: E.dma_start(
                        out=wblk[kv % 2][:, :], in_=WkT[kv * 128:(kv + 1) * 128, :]),
                        writes=[f"wblk{kv % 2}"])
                    for th in range(2):
                        pi = (kv * 2 + th) % 4
                        ps = psA[pi]
                        S.mms([(lambda E, i=i, th=th, ps=ps, kv=kv: E.matmul(
                            ps[:, :], wblk[kv % 2][:, i * 128:(i + 1) * 128],
                            RB[:, i * TPC + th * 512: i * TPC + th * 512 + 512],
                            start=(i == 0), stop=(i == NI - 1))) for i in range(NI)],
                            reads=[f"wblk{kv % 2}", "p1big"], writes=[PK[pi]])
                        S.op("vector", lambda E, ps=ps: E.tensor_scalar_mul(rot[0:64, :], ps[64:128, :], -1.0),
                             reads=[PK[pi]], writes=["rot"])
                        S.op("vector", lambda E, ps=ps: E.tensor_copy(out=rot[64:128, :], in_=ps[0:64, :]),
                             reads=[PK[pi]], writes=["rot2"])
                        S.op("vector", lambda E, ps=ps, th=th: E.tensor_mul(
                            t1b[:, :], ps[:, :], cosb[:, th * 512:th * 512 + 512]),
                            reads=[PK[pi], "cos"], writes=["t1b"])
                        S.op("vector", lambda E, th=th: E.tensor_mul(
                            t2b[:, :], rot[:, :], sinb[:, th * 512:th * 512 + 512]),
                            reads=["rot", "rot2", "sin"], writes=["t2b"])
                        S.op("vector", lambda E, kv=kv, th=th: E.tensor_add(
                            kfm[:, kv * TPC + th * 512: kv * TPC + th * 512 + 512], t1b[:, :], t2b[:, :]),
                            reads=["t1b", "t2b"], writes=["kfm"])

                for og in range(2):
                    for pas in range(2):
                        for i4 in range(NI // 4):
                            bi = i4 % 4
                            S.dma("gpsimd", lambda E, i4=i4, og=og, bi=bi: E.dma_start(
                                out=wtile[bi][:, :],
                                in_=WvT[(og * 8 + i4) * 128:(og * 8 + i4 + 1) * 128, :]),
                                writes=[f"wtile{bi}"])
                            for im in range(4):
                                i = i4 * 4 + im
                                S.mms([(lambda E, i=i, im=im, t=t, bi=bi: E.matmul(
                                    psA[t % 4][:, :], RB[:, i * TPC + t * 128: i * TPC + (t + 1) * 128],
                                    wtile[bi][:, im * 512:(im + 1) * 512], start=(i == 0), stop=(i == NI - 1)))
                                    for t in range(pas * 4, pas * 4 + 4)],
                                    reads=[f"wtile{bi}", "p1big"],
                                    writes=[PK[t % 4] for t in range(pas * 4, pas * 4 + 4)])
                        for t in range(pas * 4, pas * 4 + 4):
                            vb = t % 2
                            S.op("scalar", lambda E, t=t, vb=vb: E.copy(
                                out=vst[vb][:, :], in_=psA[t % 4][:, :]),
                                reads=[PK[t % 4]], writes=[f"vst{vb}"])
                            S.dma("sync", lambda E, t=t, og=og, vb=vb: E.dma_start(
                                out=vt_dram[:, (t * KVH + og * 4) * 128:(t * KVH + og * 4 + 4) * 128],
                                in_=vst[vb][:, :]),
                                reads=[f"vst{vb}"], nodep=["vtd"], stream="vtd")

                S.dma("gpsimd", lambda E: E.dma_start(
                    out=wblk[0][:, 0:32 * 32], in_=WinT[:, :]),
                    writes=["wblk0"])
                for th in range(2):
                    ps = psA[4 + th]
                    S.mms([(lambda E, i=i, th=th, ps=ps: E.matmul(
                        ps[0:32, :], wblk[0][:, i * 32:(i + 1) * 32],
                        RB[:, i * TPC + th * 512: i * TPC + th * 512 + 512],
                        start=(i == 0), stop=(i == NI - 1))) for i in range(NI)],
                        reads=["wblk0", "p1big"], writes=[PK[4 + th]])
                    S.op("vector", lambda E, th=th, ps=ps: E.tensor_scalar_add(
                        zdt[:, th * 512:th * 512 + 512], ps[0:32, :], dtbc[:, 0:1]),
                        reads=[PK[4 + th], "dtbc"], writes=["zdt"])
                S.op("scalar", lambda E: E.activation(out=gfm[:, :], in_=zdt[:, :], func=AF.Abs),
                     reads=["zdt"], writes=["gfm"])
                S.op("scalar", lambda E: E.activation(out=gfm[:, :], in_=gfm[:, :], func=AF.Exp, scale=-1.0),
                     reads=["gfm", "cpool"], writes=["gfm"])
                S.op("scalar", lambda E: E.activation(out=gfm[:, :], in_=gfm[:, :], func=AF.Ln, bias=1.0),
                     reads=["gfm", "cpool"], writes=["gfm"])
                S.op("vector", lambda E: E.tensor_scalar_max(dfm[:, :], zdt[:, :], 0.0),
                     reads=["zdt"], writes=["dfm"])
                S.op("vector", lambda E: E.tensor_add(dfm[:, :], dfm[:, :], gfm[:, :]),
                     reads=["dfm", "gfm"], writes=["dfm"])
                S.op("scalar", lambda E: E.activation(out=sm8[:, 40:41], in_=Albc[:, 0:1], func=AF.Exp),
                     reads=["Albc"], writes=["negA"])
                S.op("vector", lambda E: E.tensor_scalar_mul(sm8[:, 40:41], sm8[:, 40:41], -1.0),
                     reads=["negA"], writes=["negA"])
                S.op("vector", lambda E: E.tensor_scalar_mul(gfm[:, :], dfm[:, :], sm8[:, 40:41]),
                     reads=["dfm", "negA"], writes=["gfm"])
                for n in range(NCH):
                    S.op("vector", lambda E, n=n: E.tensor_tensor_scan(
                        out=Gfm[:, n * CH:(n + 1) * CH], data0=ones32[:, :], data1=gfm[:, n * CH:(n + 1) * CH],
                        initial=0.0, op0=ALU.mult, op1=ALU.add),
                        reads=["gfm", "ones32"], writes=["Gfm"])
                for n in range(NCH):
                    sl = n % 2
                    S.op("tensor", lambda E, n=n, sl=sl: E.transpose(
                        psA[6][:, sl * 128: sl * 128 + 32], Gfm[:, n * CH:(n + 1) * CH], identf[0:32, 0:32]),
                        reads=["Gfm", "identf"], writes=[PK[6]])
                    S.op("scalar", lambda E, n=n, sl=sl: E.copy(out=negGT[:, n * 32:(n + 1) * 32],
                                                                in_=psA[6][:, sl * 128: sl * 128 + 32]),
                        reads=[PK[6]], writes=["GT"])
                    S.op("tensor", lambda E, n=n, sl=sl: E.transpose(
                        psA[5][:, sl * 128: sl * 128 + 32], dfm[:, n * CH:(n + 1) * CH], identf[0:32, 0:32]),
                        reads=["dfm", "identf"], writes=[PK[5]])
                    S.op("scalar", lambda E, n=n, sl=sl: E.copy(out=dtT[:, n * 32:(n + 1) * 32],
                                                                in_=psA[5][:, sl * 128: sl * 128 + 32]),
                        reads=[PK[5]], writes=["dtT"])
                S.op("vector", lambda E: E.tensor_scalar_mul(negGT[:, :], negGT[:, :], -1.0),
                     reads=["GT"], writes=["negGT"])
                S.op("scalar", lambda E: E.activation(out=expGT[:, :], in_=negGT[:, :], func=AF.Exp, scale=-1.0),
                     reads=["negGT", "cpool"], writes=["expGT"])
                for n in range(NCH):
                    S.op("vector", lambda E, n=n: E.tensor_copy(out=sm8[:, n:n + 1],
                                                                in_=Gfm[:, n * CH + 127:n * CH + 128]),
                         reads=["Gfm"], writes=["Glast"])
                S.op("vector", lambda E: E.reduce_sum(sm8[:, 8:9], sm8[:, 0:8], axis=mybir.AxisListType.X),
                     reads=["Glast"], writes=["Gtot"])
                S.op("vector", lambda E: E.memset(sm8[:, 16:17], 0.0), writes=["Gsh"])
                S.op("vector", lambda E: E.tensor_copy(out=sm8[:, 17:24], in_=sm8[:, 0:7]),
                     reads=["Glast"], writes=["Gsh"])
                S.op("vector", lambda E: E.tensor_tensor_scan(out=sm8[:, 48:56], data0=ones32[:, 0:8],
                                                              data1=sm8[:, 16:24], initial=0.0,
                                                              op0=ALU.mult, op1=ALU.add),
                     reads=["Gsh", "ones32"], writes=["cumGx"])
                S.op("vector", lambda E: E.tensor_scalar(out=sm8[:, 24:32], in0=sm8[:, 48:56], scalar1=-1.0,
                                                         scalar2=sm8[:, 8:9], op0=ALU.mult, op1=ALU.add),
                     reads=["cumGx", "Gtot"], writes=["Sfx"])
                for (src0, src1, dst0, wkey, rkey) in (
                        (0, 8, 0, "sm8T", "Glast"), (48, 56, 32, "sm8T2", "cumGx"), (24, 32, 64, "sm8T3", "Sfx")):
                    S.op("tensor", lambda E, src0=src0, src1=src1: E.transpose(
                        psA[6][0:8, 0:32], sm8[:, src0:src1], identf[0:32, 0:32]),
                        reads=[rkey, "identf"], writes=[PK[6]])
                    S.op("scalar", lambda E, dst0=dst0: E.copy(out=sm8T[:, dst0:dst0 + 32], in_=psA[6][0:8, 0:32]),
                         reads=[PK[6]], writes=[wkey])
                S.op("tensor", lambda E: E.transpose(psA[6][0:1, 0:32], sm8[:, 8:9], identf[0:32, 0:32]),
                     reads=["Gtot", "identf"], writes=[PK[6]])
                S.op("scalar", lambda E: E.copy(out=sm8T[0:1, 96:128], in_=psA[6][0:1, 0:32]),
                     reads=[PK[6]], writes=["GtotT"])

                def bcast8(src, dst, dstkey, sub_negGT, srckeys):
                    fns = []
                    for n in range(NCH):
                        fns.append(lambda E, n=n: E.matmul(psA[6][:, n * 32:(n + 1) * 32],
                                                           oh8b[:, n * 128:(n + 1) * 128], src,
                                                           start=True, stop=True))
                    S.mms(fns, reads=["oh8b"] + srckeys, writes=[PK[6]])
                    if sub_negGT:
                        S.op("vector", lambda E: E.tensor_add(dst[:, :], psA[6][:, 0:256], negGT[:, :]),
                             reads=[PK[6], "negGT"], writes=[dstkey])
                        S.op("scalar", lambda E: E.activation(out=dst[:, :], in_=dst[:, :], func=AF.Exp),
                             reads=[dstkey], writes=[dstkey])
                    else:
                        S.op("scalar", lambda E: E.activation(out=dst[:, :], in_=psA[6][:, 0:256], func=AF.Exp),
                             reads=[PK[6]], writes=[dstkey])

                bcast8(sm8T[:, 0:32], dcolsB, "dcolsB", False, ["sm8T"])
                bcast8(sm8T[:, 0:32], edecT, "edecT", True, ["sm8T"])
                bcast8(sm8T[:, 64:96], edsfT, "edsfT", True, ["sm8T3"])
                bcast8(sm8T[:, 32:64], cumPB, "cumPB", False, ["sm8T2"])

                if DBG:
                    S.dma("gpsimd", lambda E: E.dma_start(out=dbg_kfm[:, :], in_=kfm[:, :]), reads=["kfm"], stream="dbg", nodep=["dbg"])
                    S.dma("gpsimd", lambda E: E.dma_start(out=dbg_vtok[:, :], in_=vt_dram[:, :]), reads=["vtd"], stream="dbg", nodep=["dbg"])
                    S.dma("sync", lambda E: E.dma_start(out=dbg_GT[:, :], in_=negGT[:, :]), reads=["negGT"], stream="dbg", nodep=["dbg"])
                    S.dma("sync", lambda E: E.dma_start(out=dbg_dtT[:, :], in_=dtT[:, :]), reads=["dtT"], stream="dbg", nodep=["dbg"])

                oh4p = L2.enter_context(nc.sbuf_tensor("oh4p", [32, 512], F32))
                Gm4p = [L2.enter_context(nc.sbuf_tensor(f"Gm4p{i}", [32, 512], F32)) for i in range(2)]
                dcyp = [L2.enter_context(nc.sbuf_tensor(f"dcyp{i}", [128, 512], F32)) for i in range(2)]
                dcyb = [L2.enter_context(nc.sbuf_tensor(f"dcyb{i}", [128, 512], BF16)) for i in range(2)]
                for hg in range(NHG):
                    S.dma("sync", lambda E, hg=hg: E.dma_start(
                        out=oh4p[:, :], in_=oh4_h[hg * 32:(hg + 1) * 32, :]), writes=["oh4p"])
                    for n in range(NCH):
                        di = n % 2
                        S.op("gpsimd", lambda E, n=n, di=di: E.tensor_tensor(
                            out=v34(Gm4p[di][:, :])[0:32], in0=Gfm[:, n * CH:(n + 1) * CH]
                            .rearrange("p (o e) -> p o e", o=1).broadcast_to([32, 4, 128]),
                            in1=v34(oh4p[:, :])[0:32], op=ALU.mult),
                            reads=["Gfm", "oh4p"], writes=[f"Gm4p{di}"])
                        S.mms([(lambda E, di=di: E.matmul(psA[4 + di][:, :], identf[:, :], maskbig4[:, :],
                                                          start=True, stop=False)),
                               (lambda E, di=di: E.matmul(psA[4 + di][:, :], ones32[:, :128], Gm4p[di][:, :],
                                                          start=False, stop=True))],
                              reads=["identf", "maskbig", f"Gm4p{di}", "ones32"], writes=[PK[4 + di]])
                        S.op("vector", lambda E, n=n, hg=hg, di=di: E.tensor_tensor(
                            out=v34(dcyp[di][:, :]), in0=v34(psA[4 + di][:, :]), in1=bc4(negGT, n, hg), op=ALU.add),
                            reads=[PK[4 + di], "negGT"], writes=[f"dcyp{di}"])
                        S.op("scalar", lambda E, di=di: E.activation(out=dcyb[di][:, :], in_=dcyp[di][:, :], func=AF.Exp),
                             reads=[f"dcyp{di}"], writes=[f"dcyb{di}"])
                        S.dma("sync", lambda E, hg=hg, n=n, di=di: E.dma_start(
                            out=dcy_dram[:, hg * 4096 + n * 512: hg * 4096 + (n + 1) * 512],
                            in_=dcyb[di][:, :]),
                            reads=[f"dcyb{di}"], nodep=["dcyd"], stream="dcyd")

                for hg in range(NHG):
                    kv = hg
                    for n in range(NCH):
                        ci = n % 2
                        vi = n % 3
                        S.dma("sync", lambda E, n=n, kv=kv, vi=vi: E.dma_start(
                            out=vld[vi][:, :], in_=vt_dram[:, (n * KVH + kv) * 128:(n * KVH + kv + 1) * 128]),
                            reads=["vtd"], writes=[f"vldA{vi}"])
                        S.op("tensor", lambda E, n=n, kv=kv: E.transpose(
                            psT[:, 0:128], kfm[:, kv * TPC + n * 128: kv * TPC + (n + 1) * 128], identb[:, :]),
                            reads=["kfm", "identb"], writes=["psT"])
                        S.op("scalar", lambda E, ci=ci: E.copy(out=ktr[ci][:, :], in_=psT[:, 0:128]),
                             reads=["psT"], writes=[f"ktrA{ci}"])
                        S.op("gpsimd", lambda E, n=n, hg=hg, ci=ci, vi=vi: E.tensor_tensor(
                            out=v34(xc4[ci][:, :]), in0=rep4(vld[vi][:, :]),
                            in1=bc4(dtT, n, hg), op=ALU.mult),
                            reads=[f"vldA{vi}", "dtT"], writes=[f"xc4{ci}"])
                        S.op("gpsimd", lambda E, n=n, hg=hg, ci=ci: E.tensor_tensor(
                            out=v34(kd4[ci][:, :]), in0=rep4(ktr[ci][:, :]),
                            in1=bc4(edsfT, n, hg), op=ALU.mult),
                            reads=[f"ktrA{ci}", "edsfT"], writes=[f"kd4{ci}"])
                        S.mms([(lambda E, n=n, ci=ci, hl=hl: E.matmul(
                            psA[hl][:, 0:128],
                            kd4[ci][:, hl * 128:(hl + 1) * 128], xc4[ci][:, hl * 128:(hl + 1) * 128],
                            start=(n == 0), stop=(n == NCH - 1))) for hl in range(HG)],
                            reads=[f"xc4{ci}", f"kd4{ci}"],
                            writes=[PK[hl] for hl in range(HG)])
                    for hl in range(HG):
                        S.op("vector", lambda E, hg=hg, hl=hl: E.tensor_copy(
                            out=agst[:, (hg % 2) * 512 + hl * 128:(hg % 2) * 512 + (hl + 1) * 128],
                            in_=psA[hl][:, 0:128]),
                            reads=[PK[hl]], writes=[f"agst{hg % 2}"])
                    if hg % 2 == 1:
                        h0 = (hg - 1) * HG
                        S.dma("sync", lambda E, h0=h0: E.dma_start(
                            out=ag_in[h0 * 128:(h0 + 8) * 128, :].rearrange("(a p) e -> p a e", p=128),
                            in_=agst[:, :].rearrange("p (a e) -> p a e", a=8)),
                            reads=["agst0", "agst1"], nodep=["ag_in"])
                S.dma("sync", lambda E: E.dma_start(out=ag_in[H * HD:H * HD + 1, 0:32], in_=sm8T[0:1, 96:128]),
                      reads=["GtotT"], nodep=["ag_in"])
                if not NOAG:
                    S.collective(lambda E: E.collective_compute(
                        "AllGather", ALU.bypass, replica_groups=[list(range(NCORES))],
                        ins=[ag_in_t.ap().opt()], outs=[ag_out_t.ap().opt()]),
                        reads=["ag_in"], writes=["ag_out"])
                S.flush(block)
            if STAGE <= 2:
                S.stop = True

            S.barrier()
            with contextlib.ExitStack() as L2:
                def sb2(name, shape, dtype):
                    return L2.enter_context(nc.sbuf_tensor(name, list(shape), dtype))
                qg2 = [sb2(f"qg{i}", (128, HG * TPC), BF16) for i in range(2)]
                sgl = [sb2(f"sgl{i}", (128, 512), BF16) for i in range(3)]
                sgst = [sb2("sgst0", (128, 512), BF16)]
                gnb = sb2("gnb", (128, 512), BF16)
                S.dma("gpsimd", lambda E: E.dma_start(out=gnb[:, :], in_=gnormr[:, :]), writes=["gnb"], stream="CONST")
                rot = sb2("rotm", (128, 512), F32)
                t1b = sb2("t1m", (128, 512), F32)
                t2b = sb2("t2m", (128, 512), F32)
                dcyl = [sb2(f"dcyl{i}", (128, 512), BF16) for i in range(3)]
                apt4 = [sb2(f"apt4{i}", (128, 512), BF16) for i in range(2)]
                xcm4 = [sb2(f"xcm4{i}", (128, 512), BF16) for i in range(2)]
                kdm4 = [sb2(f"kdm4{i}", (128, 512), BF16) for i in range(2)]
                ytk4 = sb2("ytk40", (128, 512), F32)
                s1b4 = sb2("s1b4", (128, 512), F32)
                ygb4x = [sb2(f"ygb4{i}", (128, 512), BF16) for i in range(2)]
                rcol4 = sb2("rcol4", (128, 12), F32)
                sdump = sb2("sdump", (128, 128), BF16)
                Sbf4 = sb2("Sbf4b", (128, 512), BF16)
                Sinsc4 = sb2("Sinsc4", (128, 512), BF16)
                Stmp = sb2("Stmp", (128, 512), F32)
                cg4 = [sb2(f"cg4{i}", (128, 512), F32) for i in range(2)]
                Cch = [sb2(f"Cch{i}", (128, 512), F32) for i in range(2)]
                ygs4 = [sb2("ygs40", (128, 512), BF16)]
                vldm = [sb2(f"vldm{i}", (128, 128), BF16) for i in range(3)]
                ktrm = [sb2(f"ktrm{i}", (128, 128), BF16) for i in range(2)]
                SinF = [Sin, Sin2]

                def proj_units(hg):
                    p = hg % 2
                    qg = qg2[p]
                    qk = f"qg{p}"
                    units = []
                    for hl in range(HG):
                        hh = hg * HG + hl
                        units.append(lambda hh=hh, hl=hl: S.dma("gpsimd", lambda E: E.dma_start(
                            out=wblk[hl % 2][:, :], in_=WqT[hh * 128:(hh + 1) * 128, :]),
                            writes=[f"wblk{hl % 2}"]))
                        p0, p1_ = (hl % 2) * 2, (hl % 2) * 2 + 1

                        def u_q(hl=hl, p0=p0, p1_=p1_, qg=qg, qk=qk):
                            S.mms([(lambda E, i=i, th=th, hl=hl: E.matmul(
                                psA[(p0, p1_)[th]][:, :], wblk[hl % 2][:, i * 128:(i + 1) * 128],
                                RB[:, i * TPC + th * 512: i * TPC + th * 512 + 512],
                                start=(i == 0), stop=(i == NI - 1))) for i in range(NI) for th in range(2)],
                                reads=[f"wblk{hl % 2}", "p1big"], writes=[PK[p0], PK[p1_]])

                        def u_rope(hl=hl, th=None, p0=p0, p1_=p1_, qg=qg, qk=qk):
                            for th in range(2):
                                pi = (p0, p1_)[th]
                                ps = psA[pi]
                                S.op("vector", lambda E, ps=ps: E.tensor_scalar_mul(rot[0:64, :], ps[64:128, :], -1.0),
                                     reads=[PK[pi]], writes=["rotm"])
                                S.op("vector", lambda E, ps=ps: E.tensor_copy(out=rot[64:128, :], in_=ps[0:64, :]),
                                     reads=[PK[pi]], writes=["rotm2"])
                                S.op("vector", lambda E, ps=ps, th=th: E.tensor_mul(
                                    t1b[:, :], ps[:, :], cosb[:, th * 512:th * 512 + 512]),
                                    reads=[PK[pi], "cos"], writes=["t1m"])
                                S.op("vector", lambda E, th=th: E.tensor_mul(
                                    t2b[:, :], rot[:, :], sinb[:, th * 512:th * 512 + 512]),
                                    reads=["rotm", "rotm2", "sin"], writes=["t2m"])
                                S.op("vector", lambda E, hl=hl, th=th, qg=qg: E.tensor_add(
                                    qg[:, hl * TPC + th * 512: hl * TPC + th * 512 + 512], t1b[:, :], t2b[:, :]),
                                    reads=["t1m", "t2m"], writes=[qk])
                        units.append(u_q)
                        units.append(u_rope)
                    for pas in range(2):
                        for i4 in range(NI // 4):
                            bi = (pas * 8 + i4) % 4

                            def u_g(i4=i4, hg=hg, bi=bi, pas=pas):
                                S.dma("gpsimd", lambda E: E.dma_start(
                                    out=wtile[bi][:, :],
                                    in_=WgT[(hg * 8 + i4) * 128:(hg * 8 + i4 + 1) * 128, :]),
                                    writes=[f"wtile{bi}"])
                                for im in range(4):
                                    i = i4 * 4 + im
                                    S.mms([(lambda E, i=i, im=im, t=t, bi=bi: E.matmul(
                                        psA[t % 4][:, :], RB[:, i * TPC + t * 128: i * TPC + (t + 1) * 128],
                                        wtile[bi][:, im * 512:(im + 1) * 512], start=(i == 0), stop=(i == NI - 1)))
                                        for t in range(pas * 4, pas * 4 + 4)],
                                        reads=[f"wtile{bi}", "p1big"],
                                        writes=[PK[t % 4] for t in range(pas * 4, pas * 4 + 4)])
                                if i4 == NI // 4 - 1:
                                    for t in range(pas * 4, pas * 4 + 4):
                                        S.op("scalar", lambda E, t=t: E.activation(
                                            out=t1b[:, :], in_=psA[t % 4][:, :], func=AF.Silu),
                                            reads=[PK[t % 4]], writes=["t1m"])
                                        S.op("vector", lambda E: E.tensor_mul(
                                            sgst[0][:, :], t1b[:, 0:512], gnb[:, :]),
                                            reads=["t1m", "gnb"], writes=["sgst0"])
                                        S.dma("sync", lambda E, hg=hg, t=t: E.dma_start(
                                            out=sg_dram[:, hg * 4096 + t * 512: hg * 4096 + (t + 1) * 512],
                                            in_=sgst[0][:, :]),
                                            reads=["sgst0"], nodep=["sgd"], stream="sgd")
                            units.append(u_g)
                    if hg == 0:
                        def u_coef():
                            for r in range(NCORES):
                                S.dma("sync", lambda E, r=r: E.dma_start(
                                    out=gt8[r:r + 1, :], in_=ag_out[r * AGR + H * HD: r * AGR + H * HD + 1, 0:32]),
                                    reads=["ag_out"], writes=["gt8"], stream="gt8")
                            S.op("vector", lambda E: E.tensor_mul(gt8[:, :], gt8[:, :], m8[:, :]),
                                 reads=["gt8", "m8"], writes=["gt8"])
                            S.mms([lambda E: E.matmul(psA[6][0:8, 0:32], trig8[:, :], gt8[:, :], start=True, stop=True)],
                                  reads=["trig8", "gt8"], writes=[PK[6]])
                            S.op("scalar", lambda E: E.activation(out=cf8[:, :], in_=psA[6][0:8, 0:32], func=AF.Exp),
                                 reads=[PK[6]], writes=["cf8"])
                            S.op("vector", lambda E: E.tensor_mul(cf8[:, :], cf8[:, :], m8[:, :]),
                                 reads=["cf8", "m8"], writes=["cf8"])
                            fns = []
                            for r in range(NCORES):
                                fns.append(lambda E, r=r: E.matmul(psA[6][:, r * 32:(r + 1) * 32],
                                                                   oh8b[:, r * 128:(r + 1) * 128], cf8[:, :],
                                                                   start=True, stop=True))
                            S.mms(fns, reads=["oh8b", "cf8"], writes=[PK[6]])
                            S.op("vector", lambda E: E.tensor_copy(out=coefB[:, :], in_=psA[6][:, 0:256]),
                                 reads=[PK[6]], writes=["coefB"])
                        units.append(u_coef)
                    for r in range(NCORES):
                        def u_c(r=r, hg=hg, p=p):
                            ci = r % 2
                            S.dma("sync", lambda E: E.dma_start(
                                out=cg4[ci][:, :].rearrange("p (hl e) -> p hl e", hl=4),
                                in_=ag_out[r * AGR + hg * 512: r * AGR + (hg + 1) * 512, :]
                                .rearrange("(hl p) e -> p hl e", p=128)),
                                reads=["ag_out"], writes=[f"cg4{ci}"])
                            c0 = r * 32 + 4 * hg
                            cb = coefB[:, c0:c0 + 4].rearrange("p (h o) -> p h o", o=1).broadcast_to([128, 4, 128])
                            if r == 0:
                                S.op("gpsimd", lambda E: E.tensor_tensor(
                                    out=v34(Cch[0][:, :]), in0=v34(cg4[ci][:, :]), in1=cb, op=ALU.mult),
                                    reads=[f"cg4{ci}", "coefB"], writes=["Cch0"])
                            else:
                                S.op("gpsimd", lambda E: E.tensor_tensor(
                                    out=v34(Cch[1][:, :]), in0=v34(cg4[ci][:, :]), in1=cb, op=ALU.mult),
                                    reads=[f"cg4{ci}", "coefB"], writes=["Cch1"])
                                accb, acck = (Cch[0], "Cch0") if r % 2 == 1 else (SinF[p], f"SinF{p}")
                                dstb, dk = (SinF[p], f"SinF{p}") if r % 2 == 1 else (Cch[0], "Cch0")
                                S.op("gpsimd", lambda E, dstb=dstb, accb=accb: E.tensor_add(
                                    dstb[:, :], Cch[1][:, :], accb[:, :]),
                                    reads=["Cch1", acck], writes=[dk])
                        units.append(u_c)
                    if DBG:
                        units.append(lambda hg=hg, p=p: (
                            S.dma("gpsimd", lambda E: E.dma_start(out=dbg_q[hg * 128:(hg + 1) * 128, :],
                                                                  in_=qg2[p][:, :]),
                                  reads=[f"qg{p}"], stream="dbg", nodep=["dbg"]),
                            S.dma("gpsimd", lambda E: E.dma_start(out=dbg_sg[:, hg * 4096:(hg + 1) * 4096],
                                                                  in_=sg_dram[:, hg * 4096:(hg + 1) * 4096]),
                                  reads=["sgd"], stream="dbg", nodep=["dbg"]),
                            S.dma("sync", lambda E: E.dma_start(out=dbg_Sin[hg * 128:(hg + 1) * 128, :],
                                                                in_=SinF[p][:, :]),
                                  reads=[f"SinF{p}"], stream="dbg", nodep=["dbg"])))
                    return units

                def mixer_iter_steps(hg, n):
                  def _stepA(hg=hg, n=n):
                    p = hg % 2
                    qg = qg2[p]
                    qk = f"qg{p}"
                    Sprev, Snext = (SA, SB_) if n % 2 == 0 else (SB_, SA)
                    spk, snk = ("SA", "SB_") if n % 2 == 0 else ("SB_", "SA")
                    kv = hg
                    it = n % 2
                    vi = n % 3
                    if n == 0:
                        S.op("vector", lambda E: E.memset(SA[:, :], 0.0), writes=["SA"])
                    ktile = kfm[:, kv * TPC + n * 128: kv * TPC + (n + 1) * 128]
                    S.dma("sync", lambda E, n=n, kv=kv, vi=vi: E.dma_start(
                        out=vldm[vi][:, :], in_=vt_dram[:, (n * KVH + kv) * 128:(n * KVH + kv + 1) * 128]),
                        reads=["vtd"], writes=[f"vldm{vi}"])
                    S.dma("sync", lambda E, n=n, hg=hg, vi=vi: E.dma_start(
                        out=sgl[vi][:, :], in_=sg_dram[:, hg * 4096 + n * 512: hg * 4096 + (n + 1) * 512]),
                        reads=["sgd"], writes=[f"sgl{vi}"])
                    S.mms([lambda E, ktile=ktile: E.transpose(
                        psA[5][:, 0:64].bitcast(BF16), ktile, identb[:, :])],
                        reads=["kfm", "identb"], writes=[PK[5]])
                    S.op("scalar", lambda E, it=it: E.copy(out=ktrm[it][:, :], in_=psA[5][:, 0:64].bitcast(BF16)),
                         reads=[PK[5]], writes=[f"ktrm{it}"])
                    S.dma("sync", lambda E, n=n, hg=hg, vi=vi: E.dma_start(
                        out=dcyl[vi][:, :], in_=dcy_dram[:, hg * 4096 + n * 512: hg * 4096 + (n + 1) * 512]),
                        reads=["dcyd"], writes=[f"dcyl{vi}"])
                    S.mms([(lambda E, hl=hl, n=n, ktile=ktile, qg=qg: E.matmul(
                        psA[5][:, hl * 128:(hl + 1) * 128], ktile,
                        qg[:, hl * TPC + n * 128: hl * TPC + (n + 1) * 128],
                        start=True, stop=True)) for hl in range(HG)],
                        reads=["kfm", qk], writes=[PK[5]])
                    S.op("vector", lambda E, it=it, vi=vi: E.tensor_mul(apt4[it][:, :], psA[5][:, :], dcyl[vi][:, :]),
                         reads=[PK[5], f"dcyl{vi}"], writes=[f"apt4{it}"])
                    S.op("vector", lambda E, n=n, hg=hg, it=it, vi=vi: E.tensor_tensor(
                        out=v34(xcm4[it][:, :]), in0=rep4(vldm[vi][:, :]), in1=bc4(dtT, n, hg), op=ALU.mult),
                        reads=[f"vldm{vi}", "dtT"], writes=[f"xcm4{it}"])
                    S.op("vector", lambda E, n=n, hg=hg, it=it: E.tensor_tensor(
                        out=v34(kdm4[it][:, :]), in0=rep4(ktrm[it][:, :]), in1=bc4(edecT, n, hg), op=ALU.mult),
                        reads=[f"ktrm{it}", "edecT"], writes=[f"kdm4{it}"])
                    S.op("gpsimd", lambda E, n=n, hg=hg, p=p: E.tensor_tensor(
                        out=v34(Sinsc4[:, :]), in0=v34(SinF[p][:, :]), in1=bc4(cumPB, n, hg), op=ALU.mult),
                        reads=[f"SinF{p}", "cumPB"], writes=["Sinsc4"])
                    S.op("scalar", lambda E, Sprev=Sprev: E.copy(out=Sbf4[:, :], in_=Sprev[:, :]),
                         reads=[spk], writes=["Sbf4"])

                  def _stepB(hg=hg, n=n):
                    p = hg % 2
                    qg = qg2[p]
                    qk = f"qg{p}"
                    Sprev, Snext = (SA, SB_) if n % 2 == 0 else (SB_, SA)
                    spk, snk = ("SA", "SB_") if n % 2 == 0 else ("SB_", "SA")
                    kv = hg
                    it = n % 2
                    vi = n % 3
                    S.mms([(lambda E, hl=hl, it=it: E.matmul(
                        psA[6][:, hl * 128:(hl + 1) * 128], apt4[it][:, hl * 128:(hl + 1) * 128],
                        xcm4[it][:, hl * 128:(hl + 1) * 128], start=True, stop=True)) for hl in range(HG)],
                        reads=[f"apt4{it}", f"xcm4{it}"], writes=[PK[6]])
                    fns = []
                    for hl in range(HG):
                        fns.append(lambda E, hl=hl, n=n, qg=qg: E.matmul(
                            psA[4][:, hl * 128:(hl + 1) * 128],
                            qg[:, hl * TPC + n * 128: hl * TPC + (n + 1) * 128],
                            Sbf4[:, hl * 128:(hl + 1) * 128], start=True, stop=False))
                        fns.append(lambda E, hl=hl, n=n, qg=qg: E.matmul(
                            psA[4][:, hl * 128:(hl + 1) * 128],
                            qg[:, hl * TPC + n * 128: hl * TPC + (n + 1) * 128],
                            Sinsc4[:, hl * 128:(hl + 1) * 128], start=False, stop=True))
                    S.mms(fns, reads=[qk, "Sbf4", "Sinsc4"], writes=[PK[4]])
                    S.op("vector", lambda E, n=n, hg=hg: E.tensor_tensor(
                        out=v34(s1b4[:, :]), in0=v34(psA[4][:, :]), in1=bc4(expGT, n, hg), op=ALU.mult),
                        reads=[PK[4], "expGT"], writes=["s1b4"])
                    S.op("vector", lambda E: E.tensor_add(ytk4[:, :], s1b4[:, :], psA[6][:, :]),
                         reads=["s1b4", PK[6]], writes=["ytk40"])
                    S.mms([(lambda E, hl=hl, it=it: E.matmul(
                        psA[6][:, hl * 128:(hl + 1) * 128], kdm4[it][:, hl * 128:(hl + 1) * 128],
                        xcm4[it][:, hl * 128:(hl + 1) * 128], start=True, stop=True)) for hl in range(HG)],
                        reads=[f"kdm4{it}", f"xcm4{it}"], writes=[PK[6]])
                    for hl in range(HG):
                        S.op("scalar", lambda E, hl=hl: E.activation(
                            out=sdump[:, :], in_=ytk4[:, hl * 128:(hl + 1) * 128], func=AF.Square,
                            accum_out=rcol4[:, hl:hl + 1]),
                            reads=["ytk40"], writes=["sdump", "rsum"])
                    S.op("scalar", lambda E: E.activation(out=rcol4[:, 4:8], in_=rcol4[:, 0:4], func=AF.Sqrt,
                                                          scale=1.0 / HD, bias=float(RMS_EPS)),
                         reads=["rsum", "cpool"], writes=["rms4"])
                    S.op("vector", lambda E: E.reciprocal(rcol4[:, 8:12], rcol4[:, 4:8]),
                         reads=["rms4"], writes=["rstd4"])
                    S.op("vector", lambda E: E.tensor_tensor(
                        out=v34(Stmp[:, :]), in0=v34(ytk4[:, :]),
                        in1=rcol4[:, 8:12].rearrange("p (h o) -> p h o", o=1).broadcast_to([128, 4, 128]),
                        op=ALU.mult),
                        reads=["ytk40", "rstd4"], writes=["Stmp"])
                    S.op("vector", lambda E, it=it, vi=vi: E.tensor_mul(
                        ygb4x[it][:, :], Stmp[:, :], sgl[vi][:, :]),
                        reads=["Stmp", f"sgl{vi}"], writes=[f"ygb4{it}"])
                    S.op("vector", lambda E, n=n, hg=hg, Sprev=Sprev: E.tensor_tensor(
                        out=v34(s1b4[:, :]), in0=v34(Sprev[:, :]), in1=bc4(dcolsB, n, hg), op=ALU.mult),
                        reads=[spk, "dcolsB"], writes=["s1b4"])
                    S.op("vector", lambda E, Snext=Snext: E.tensor_add(Snext[:, :], s1b4[:, :], psA[6][:, :]),
                         reads=["s1b4", PK[6]], writes=[snk])
                  def _stepC(hg=hg, n=n):
                    it = n % 2
                    if True:
                        S.mms([(lambda E, hl=hl: E.transpose(
                            psT[:, hl * 128:(hl + 1) * 128], ygb4x[it][:, hl * 128:(hl + 1) * 128],
                            identb[:, :])) for hl in range(HG)],
                            reads=[f"ygb4{it}", "identb"], writes=["psT"])
                        S.op("scalar", lambda E: E.copy(out=ygs4[0][:, :], in_=psT[:, 0:512]),
                             reads=["psT"], writes=["ygs40"])
                        S.dma("sync", lambda E, hg=hg, n=n: E.dma_start(
                            out=ygT_dram[hg * 512:(hg + 1) * 512, n * 128:(n + 1) * 128]
                            .rearrange("(hl p) t -> p hl t", p=128),
                            in_=ygs4[0][:, :].rearrange("p (hl t) -> p hl t", hl=4)),
                            reads=["ygs40"], nodep=["ygd"], stream="ygd")
                  return [_stepA, _stepB, _stepC]

                for hg in range(NHG):
                    units = proj_units(hg)
                    steps = []
                    if hg > 0:
                        for n in range(NCH):
                            steps.extend(mixer_iter_steps(hg - 1, n))
                    nu, ns = len(units), len(steps)
                    si = 0
                    for j, u in enumerate(units):
                        u()
                        while si < ns and si < (j + 1) * ns // nu:
                            steps[si]()
                            si += 1
                    while si < ns:
                        steps[si]()
                        si += 1
                for n in range(NCH):
                    for st_ in mixer_iter_steps(NHG - 1, n):
                        st_()
                S.flush(block)
        if STAGE <= 4:
            S.stop = True

        S.barrier()
        with contextlib.ExitStack() as L1:
            wtile = [L1.enter_context(nc.sbuf_tensor(f"wo{i}", [128, 2048], BF16)) for i in range(3)]
            xrl = [L1.enter_context(nc.sbuf_tensor(f"xrl{i}", [128, 2, 512], F32)) for i in range(4)]
            x2st = [L1.enter_context(nc.sbuf_tensor(f"x2st{i}", [128, 512], F32)) for i in range(2)]
            for tc in range(NCH):
                if tc == 0:
                    S.dma("sync", lambda E, tc=tc: E.dma_start(
                        out=RB[:, tc * 4096:(tc + 1) * 4096].rearrange("p (m o) -> p m o", m=32),
                        in_=ygT_dram[:, tc * 128:(tc + 1) * 128].rearrange("(m p) o -> p m o", p=128)),
                        reads=["ygd"], writes=["p1big"], nodep=["ygres"], stream="ygres")
                else:
                    S.dma("sync", lambda E, tc=tc: E.dma_start(
                        out=RB[:, tc * 4096:(tc + 1) * 4096].rearrange("p (m o) -> p m o", m=32),
                        in_=ygT_dram[:, tc * 128:(tc + 1) * 128].rearrange("(m p) o -> p m o", p=128)),
                        reads=["ygd"], nodep=["ygres"], stream="ygres")
            psd = [psA[i][:, :] for i in range(7)] + [psT[:, :].bitcast(F32)]
            pkd = PK + ["psT"]

            def ld_wo(og, m4):
                bi = (og * 8 + m4) % 3
                S.dma("gpsimd", lambda E, m4=m4, og=og, bi=bi: E.dma_start(
                    out=wtile[bi][:, :],
                    in_=WoT[(og * 8 + m4) * 128:(og * 8 + m4 + 1) * 128, :]),
                    writes=[f"wo{bi}"])

            flatw = [(og, m4) for og in range(8) for m4 in range(NI // 4)]
            for j in range(2):
                ld_wo(*flatw[j])
            for idx, (og, m4) in enumerate(flatw):
                if idx + 2 < len(flatw):
                    ld_wo(*flatw[idx + 2])
                bi = (og * 8 + m4) % 3
                for mm in range(4):
                    m = m4 * 4 + mm
                    S.mms([(lambda E, m=m, mm=mm, t=t, bi=bi: E.matmul(
                        psd[t], RB[:, t * 4096 + m * 128: t * 4096 + (m + 1) * 128],
                        wtile[bi][:, mm * 512:(mm + 1) * 512], start=(m == 0), stop=(m == NI - 1)))
                        for t in range(8)],
                        reads=[f"wo{bi}", "ygres"],
                        writes=pkd)
                if m4 == 5:
                    for t in range(8):
                        S.dma("sync", lambda E, t=t, og=og: E.dma_start(
                            out=xrl[t % 4][:, t // 4, :], in_=x[t * 128:(t + 1) * 128, og * 512:(og + 1) * 512]),
                            writes=[f"xrl{t % 4}"])
                if m4 == NI // 4 - 1:
                    for t in range(8):
                        bo = t % 2
                        S.op("vector", lambda E, t=t, bo=bo: E.tensor_add(
                            x2st[bo][:, :], psd[t], xrl[t % 4][:, t // 4, :]),
                            reads=[pkd[t], f"xrl{t % 4}"], writes=[f"x2st{bo}"])
                        S.dma("sync", lambda E, t=t, og=og, bo=bo: E.dma_start(
                            out=x2_dram[t * 128:(t + 1) * 128, og * 512:(og + 1) * 512], in_=x2st[bo][:, :]),
                            reads=[f"x2st{bo}"], nodep=["x2d"])
            S.flush(block)

        S.barrier()
        with contextlib.ExitStack() as L1:
            xt = [L1.enter_context(nc.sbuf_tensor(f"x2t{i}", [128, DM], F32)) for i in range(2)]
            xnt = L1.enter_context(nc.sbuf_tensor("xn2t", [128, DM], BF16))
            cols = L1.enter_context(nc.sbuf_tensor("cols2", [128, 8], F32))
            S.wr["p1big"] = S.wr.get("ygres")
            ln_transpose(x2_dram, "x2d", xt, xnt, cols, "xn2big", "p2")
            S.flush(block)
        if STAGE <= 5:
            S.stop = True

        S.barrier()
        with contextlib.ExitStack() as L1:
            wga = [L1.enter_context(nc.sbuf_tensor(f"wga{i}", [128, 32 * 128], BF16)) for i in range(2)]
            wup = [L1.enter_context(nc.sbuf_tensor(f"wup{i}", [128, 32 * 128], BF16)) for i in range(2)]
            gsb = L1.enter_context(nc.sbuf_tensor("gsb", [128, 1024], BF16))
            usb = L1.enter_context(nc.sbuf_tensor("usb", [128, 1024], BF16))
            hsb = [L1.enter_context(nc.sbuf_tensor(f"hsb{i}", [128, 1024], BF16)) for i in range(2)]
            sgc = L1.enter_context(nc.sbuf_tensor("sgc", [128, NKI], F32))
            suc = L1.enter_context(nc.sbuf_tensor("suc", [128, NKI], F32))
            S.dma("sync", lambda E: E.dma_start(out=sgc[:, :], in_=sgate2[:, :]), writes=["sgc"], stream="CONST")
            S.dma("sync", lambda E: E.dma_start(out=suc[:, :], in_=sup2[:, :]), writes=["suc"], stream="CONST")

            for oc in range(NKI):
                bi = oc % 2
                S.dma("gpsimd", lambda E, oc=oc, bi=bi: E.dma_start(
                    out=wga[bi][:, :], in_=WgaT[oc * 128:(oc + 1) * 128, :]),
                    writes=[f"wga{bi}"])
                S.dma("gpsimd", lambda E, oc=oc, bi=bi: E.dma_start(
                    out=wup[bi][:, :], in_=WupT[oc * 128:(oc + 1) * 128, :]),
                    writes=[f"wup{bi}"])
                S.mms([(lambda E, i=i, th=th, bi=bi: E.matmul(
                    psA[th][:, :], wga[bi][:, i * 128:(i + 1) * 128],
                    RB[:, i * TPC + th * 512: i * TPC + th * 512 + 512],
                    start=(i == 0), stop=(i == NI - 1))) for i in range(NI) for th in range(2)],
                    reads=[f"wga{bi}", "xn2big"], writes=[PK[0], PK[1]])
                S.mms([(lambda E, i=i, th=th, bi=bi: E.matmul(
                    psA[2 + th][:, :], wup[bi][:, i * 128:(i + 1) * 128],
                    RB[:, i * TPC + th * 512: i * TPC + th * 512 + 512],
                    start=(i == 0), stop=(i == NI - 1))) for i in range(NI) for th in range(2)],
                    reads=[f"wup{bi}", "xn2big"], writes=[PK[2], PK[3]])
                for th in range(2):
                    S.op("scalar", lambda E, th=th, oc=oc: E.activation(
                        out=gsb[:, th * 512:th * 512 + 512], in_=psA[th][:, :], func=AF.Silu,
                        scale=sgc[:, oc:oc + 1]),
                        reads=[PK[th], "sgc"], writes=["gsb"])
                    S.op("vector", lambda E, th=th, oc=oc: E.tensor_scalar_mul(
                        usb[:, th * 512:th * 512 + 512], psA[2 + th][:, :], suc[:, oc:oc + 1]),
                        reads=[PK[2 + th], "suc"], writes=["usb"])
                hb = oc % 2
                S.op("vector", lambda E, hb=hb: E.tensor_mul(hsb[hb][:, :], gsb[:, :], usb[:, :]),
                     reads=["gsb", "usb"], writes=[f"hsb{hb}"])
                S.dma("sync", lambda E, oc=oc, hb=hb: E.dma_start(
                    out=h_dram[oc * 128:(oc + 1) * 128, :], in_=hsb[hb][:, :]),
                    reads=[f"hsb{hb}"], nodep=["hd"])
            S.flush(block)
        if STAGE <= 6:
            S.stop = True

        S.barrier()
        with contextlib.ExitStack() as L1:
            hbf = [L1.enter_context(nc.sbuf_tensor(f"hbf{i}", [128, 1024], BF16)) for i in range(4)]
            wdn = [L1.enter_context(nc.sbuf_tensor(f"wdn{i}", [128, 1024], BF16)) for i in range(3)]
            sdb = L1.enter_context(nc.sbuf_tensor("sdb", [128, DM], F32))
            x2r = [L1.enter_context(nc.sbuf_tensor(f"x2r{i}", [128, 2, 512], F32)) for i in range(4)]
            ost = [L1.enter_context(nc.sbuf_tensor(f"ost{i}", [128, 512], F32)) for i in range(2)]
            S.dma("sync", lambda E: E.dma_start(out=sdb[:, :], in_=sdownb[:, :]), writes=["sdb"], stream="CONST")

            psd = [psA[i][:, :] for i in range(7)] + [psT[:, :].bitcast(F32)]
            pkd = PK + ["psT"]

            def ld_hbf(og, k):
                bi = (og * NKI + k) % 4
                S.dma("sync", lambda E, k=k, bi=bi: E.dma_start(
                    out=hbf[bi][:, :], in_=h_dram[k * 128:(k + 1) * 128, :]),
                    reads=["hd"], writes=[f"hbf{bi}"])

            def ld_wdn(og, k2):
                wi = (og * 43 + k2) % 3
                S.dma("gpsimd", lambda E, k2=k2, og=og, wi=wi: E.dma_start(
                    out=wdn[wi][:, :],
                    in_=WdnT[(og * 43 + k2) * 128:(og * 43 + k2 + 1) * 128, :]),
                    writes=[f"wdn{wi}"])

            flat = [(og, k) for og in range(8) for k in range(NKI)]
            flat2 = [(og, k2) for og in range(8) for k2 in range(43)]
            LOOK = 3
            for j in range(LOOK):
                ld_hbf(*flat[j])
            for j in range(2):
                ld_wdn(*flat2[j])
            w2i = 2
            for idx, (og, k) in enumerate(flat):
                if idx + LOOK < len(flat):
                    ld_hbf(*flat[idx + LOOK])
                if k % 2 == 0 and w2i < len(flat2):
                    ld_wdn(*flat2[w2i])
                    w2i += 1
                bi = (og * NKI + k) % 4
                wi = (og * 43 + k // 2) % 3
                if k == 0:
                    for t in range(8):
                        S.mms([(lambda E, t=t, bi=bi, wi=wi: E.matmul(
                            psd[t], hbf[bi][:, t * 128:(t + 1) * 128],
                            wdn[wi][:, 0:512], start=True, stop=False))],
                            reads=[f"hbf{bi}", f"wdn{wi}"], writes=[pkd[t]])
                else:
                    S.mms([(lambda E, k=k, t=t, bi=bi, wi=wi: E.matmul(
                        psd[t], hbf[bi][:, t * 128:(t + 1) * 128],
                        wdn[wi][:, (k % 2) * 512:(k % 2 + 1) * 512], start=False, stop=(k == NKI - 1)))
                        for t in range(8)],
                        reads=[f"hbf{bi}", f"wdn{wi}"],
                        writes=pkd)
                if k == 40:
                    for t in range(8):
                        S.dma("sync", lambda E, t=t, og=og: E.dma_start(
                            out=x2r[t % 4][:, t // 4, :], in_=x2_dram[t * 128:(t + 1) * 128, og * 512:(og + 1) * 512]),
                            reads=["x2d"], writes=[f"x2r{t % 4}"])
                if k == NKI - 1:
                    for t in range(8):
                        bo = t % 2
                        S.op("vector", lambda E, t=t, og=og, bo=bo: E.tensor_mul(
                            ost[bo][:, :], psd[t], sdb[:, og * 512:(og + 1) * 512]),
                            reads=[pkd[t], "sdb"], writes=[f"ost{bo}"])
                        S.op("vector", lambda E, t=t, bo=bo: E.tensor_add(
                            ost[bo][:, :], ost[bo][:, :], x2r[t % 4][:, t // 4, :]),
                            reads=[f"ost{bo}", f"x2r{t % 4}"], writes=[f"ost{bo}"])
                        S.dma("sync", lambda E, t=t, og=og, bo=bo: E.dma_start(
                            out=out[t * 128:(t + 1) * 128, og * 512:(og + 1) * 512], in_=ost[bo][:, :]),
                            reads=[f"ost{bo}"], nodep=["outd"])
            if not S.stop:
                ent = S.dstream["outd@sync"]
                S.raw("sync", lambda E, ent=ent: E.wait_ge(ent[0], ent[1]))
            S.flush(block)

    return nc


def _rope_tables(t0):
    inv_freq = 1.0 / 10000.0 ** (np.arange(0, HD, 2, dtype=np.float64) / HD)
    ang = np.arange(t0, t0 + TPC, dtype=np.float64)[:, None] * inv_freq
    cos = np.concatenate([np.cos(ang), np.cos(ang)], axis=-1)
    sin = np.concatenate([np.sin(ang), np.sin(ang)], axis=-1)
    return (np.ascontiguousarray(cos.T.astype(np.float32)),
            np.ascontiguousarray(sin.T.astype(np.float32)))


def kernel(L0_attn_hidden_states, L0_attn_Wq, L0_attn_Wk, L0_attn_Wv, L0_attn_Wo,
           L0_attn_Wg, L0_attn_Win, L0_attn_dt_bias, L0_attn_A_log_bias, L0_attn_gnorm_w,
           L0_ffn_w_gate, L0_ffn_s_gate, L0_ffn_w_up, L0_ffn_s_up, L0_ffn_w_down, L0_ffn_s_down):
    x = np.asarray(L0_attn_hidden_states, np.float32)
    shared = {
        "WqT": np.ascontiguousarray(np.asarray(L0_attn_Wq, np.float32)
                                    .reshape(32, 128, 32, 128).transpose(0, 3, 2, 1).reshape(DM, DM)),
        "WkT": np.ascontiguousarray(np.asarray(L0_attn_Wk, np.float32)
                                    .reshape(8, 128, 32, 128).transpose(0, 3, 2, 1).reshape(KVH * HD, DM)),
        "WvT": np.ascontiguousarray(np.asarray(L0_attn_Wv, np.float32).T
                                    .reshape(8, 4, 128, 2, 512).transpose(3, 0, 2, 1, 4).reshape(2048, 2048)),
        "WgT": np.ascontiguousarray(np.asarray(L0_attn_Wg, np.float32).T
                                    .reshape(8, 4, 128, 8, 512).transpose(3, 0, 2, 1, 4).reshape(8192, 2048)),
        "WoT": np.ascontiguousarray(np.asarray(L0_attn_Wo, np.float32).T
                                    .reshape(8, 4, 128, 8, 512).transpose(3, 0, 2, 1, 4).reshape(8192, 2048)),
        "WinT": np.ascontiguousarray(np.asarray(L0_attn_Win, np.float32)
                                     .reshape(H, 32, 128).transpose(2, 1, 0).reshape(128, 32 * H)),
        "WgaT": np.ascontiguousarray(np.asarray(L0_ffn_w_gate, np.int32)
                                     .reshape(NKI, 128, 32, 128).transpose(0, 3, 2, 1).reshape(INTER, DM)),
        "WupT": np.ascontiguousarray(np.asarray(L0_ffn_w_up, np.int32)
                                     .reshape(NKI, 128, 32, 128).transpose(0, 3, 2, 1).reshape(INTER, DM)),
        "WdnT": np.ascontiguousarray(np.asarray(L0_ffn_w_down, np.int32).T
                                     .reshape(43, 2, 128, 8, 512).transpose(3, 0, 2, 1, 4).reshape(44032, 1024)),
        "sgate2": np.ascontiguousarray(np.asarray(L0_ffn_s_gate, np.float32).reshape(NKI, 128).T),
        "sup2": np.ascontiguousarray(np.asarray(L0_ffn_s_up, np.float32).reshape(NKI, 128).T),
        "sdownb": np.ascontiguousarray(np.broadcast_to(np.asarray(L0_ffn_s_down, np.float32), (128, DM))),
        "gnormr": np.ascontiguousarray(np.broadcast_to(
            np.tile(np.asarray(L0_attn_gnorm_w, np.float32), 4), (128, 512))),
        "dtb": np.ascontiguousarray(np.asarray(L0_attn_dt_bias, np.float32).reshape(H, 1)),
        "Alb": np.ascontiguousarray(np.asarray(L0_attn_A_log_bias, np.float32).reshape(H, 1)),
        "identf_h": np.eye(128, dtype=np.float32),
        "ones32_h": np.ones((32, CH), np.float32),
        "trig8_h": np.triu(np.ones((8, 8), np.float32), 1).T.copy(),
        "cpool_h": np.ascontiguousarray(np.broadcast_to(np.array(CVALS, np.float32), (128, 4))),
    }
    mb = np.full((128, 128), -1.0e6, np.float32)
    iu = np.arange(128)
    mb[iu[:, None] <= iu[None, :]] = 0.0
    shared["maskbig4_h"] = np.ascontiguousarray(np.tile(mb, (1, 4)))
    oh4 = np.zeros((NHG * 32, 512), np.float32)
    for hg in range(NHG):
        for hl in range(HG):
            oh4[hg * 32 + hg * 4 + hl, hl * 128:(hl + 1) * 128] = 1.0
    shared["oh4_h"] = oh4
    oh8 = np.zeros((8, 8 * 128), np.float32)
    for r in range(8):
        oh8[r, r * 128:(r + 1) * 128] = 1.0
    shared["oh8_h"] = oh8

    in_maps = []
    for c in range(NCORES):
        b, ci = c // 4, c % 4
        t0 = ci * TPC
        cos, sin = _rope_tables(t0)
        m = np.zeros((NCORES, H), np.float32)
        for r in range(NCORES):
            if r // 4 == b and r < c:
                m[r, :] = 1.0
        d = dict(shared)
        d["x"] = np.ascontiguousarray(x[b, t0:t0 + TPC])
        d["cosr"] = cos
        d["sinr"] = sin
        d["cmask"] = m
        in_maps.append(d)

    nc = build_nc()
    res = run_bass_kernel_spmd(nc, in_maps, core_ids=list(range(NCORES)))
    if DBG:
        kernel._last = res
    outs = [np.asarray(res.results[c]["out"], np.float32) for c in range(NCORES)]
    full = np.empty((B, L, DM), np.float32)
    for c in range(NCORES):
        b, ci = c // 4, c % 4
        full[b, ci * TPC:(ci + 1) * TPC] = outs[c]
    return full
